# version 04
# baseline (speedup 1.0000x reference)
import numpy as np
import ml_dtypes
from contextlib import ExitStack

import concourse.bass as bass
import concourse.mybir as mybir
from concourse.bass_utils import run_bass_kernel_spmd

F32 = mybir.dt.float32
BF16 = mybir.dt.bfloat16
I32 = mybir.dt.int32
AF = mybir.ActivationFunctionType
ALU = mybir.AluOpType
AX = mybir.AxisListType

N_CORES = 8
T = 2048
D = 1024
NF = D // 128
NCH = 4
TC = 512

ENGINES = ("sync", "scalar", "vector", "tensor", "gpsimd")


class Prog:

    def __init__(self, nc, es):
        self.nc = nc
        self.es = es
        self.ops = {e: [] for e in ENGINES}
        self.sem = {}
        self.cnt = {}
        self.aux = {}
        self.slot_rel = {}
        self.wbuf_rel = {}

    def new_sem(self, name):
        h = self.es.enter_context(self.nc.semaphore(name))
        self.sem[name] = h
        self.cnt[name] = 0
        return h

    def n(self, name):
        return self.cnt[name]

    def E(self, eng, fn, waits=(), inc=None, amt=1):
        w = [(self.sem[s], v) for s, v in waits if v > 0]
        pair = None
        if inc is not None:
            self.cnt[inc] += amt
            pair = (self.sem[inc], amt)

        def closure(e, w=w, fn=fn, pair=pair):
            for h, v in w:
                e.wait_ge(h, v)
            inst = fn(e) if fn is not None else None
            if pair is not None:
                if inst is None:
                    e.sem_inc(pair[0], pair[1])
                else:
                    inst.then_inc(pair[0], pair[1])
        self.ops[eng].append(closure)

    def W(self, eng, waits):
        self.E(eng, None, waits=waits)

    def EV(self, eng, fn, waits=()):
        prev = self.cnt["sVV"]
        self.E(eng, fn, waits=list(waits) + [("sVV", prev)], inc="sVV")

    def finalize(self):
        nc = self.nc
        with nc.Block() as block:
            @block.sync
            def _(e):
                for f in self.ops["sync"]:
                    f(e)

            @block.scalar
            def _(e):
                for f in self.ops["scalar"]:
                    f(e)

            @block.vector
            def _(e):
                for f in self.ops["vector"]:
                    f(e)

            @block.tensor
            def _(e):
                for f in self.ops["tensor"]:
                    f(e)

            @block.gpsimd
            def _(e):
                for f in self.ops["gpsimd"]:
                    f(e)


def f32(x):
    return np.ascontiguousarray(np.asarray(x), dtype=np.float32)


def wprep(W, kin, nout):
    W = f32(W)
    return np.ascontiguousarray(
        W.reshape(kin, 128, nout, 128).transpose(2, 1, 0, 3).reshape(nout * 128, kin * 128))


def pack_vecs(vecs):
    cols = []
    for v in vecs:
        cols.append(np.asarray(v, np.float32).reshape(NF, 128).T)
    return np.ascontiguousarray(np.concatenate(cols, axis=1))



def build_l0_attn():
    nc = bass.Bass("TRN2", target_bir_lowering=False)
    es = ExitStack()
    p = Prog(nc, es)

    x0 = nc.dram_tensor("x0", [D, T], F32, kind="ExternalInput").ap()
    wk = nc.dram_tensor("wk", [D, D], F32, kind="ExternalInput").ap()
    wv = nc.dram_tensor("wv", [D, D], F32, kind="ExternalInput").ap()
    wr = nc.dram_tensor("wr", [D, D], F32, kind="ExternalInput").ap()
    wo = nc.dram_tensor("wo", [D, D], F32, kind="ExternalInput").ap()
    cwk = nc.dram_tensor("cwk", [4 * D, D], F32, kind="ExternalInput").ap()
    cwv = nc.dram_tensor("cwv", [D, 4 * D], F32, kind="ExternalInput").ap()
    cwr = nc.dram_tensor("cwr", [D, D], F32, kind="ExternalInput").ap()
    vpack = nc.dram_tensor("vpack", [128, 88], F32, kind="ExternalInput").ap()
    cvals = nc.dram_tensor("cvals", [128, 4], F32, kind="ExternalInput").ap()
    cones = nc.dram_tensor("cones", [128, 128], F32, kind="ExternalInput").ap()
    x1 = nc.dram_tensor("x1", [D, T], F32, kind="ExternalOutput").ap()

    sb = lambda name, shape, dt: es.enter_context(nc.sbuf_tensor(name, shape, dt))
    u = [sb(f"u{i}", [128, T], F32) for i in range(NF)]
    hfull = sb("hfull", [128, NF * (T + 1)], BF16)
    mixa = sb("mixa", [128, NF * T], BF16)
    eka = sb("eka", [128, NF * T], BF16)
    ekva = sb("ekva", [128, NF * T], BF16)
    dtmp2 = sb("dtmp2", [128, 2 * T], BF16)
    dtmp = dtmp2[:, 0:T]
    alre = dtmp2[:, :].bitcast(F32)
    srtb = [sb(f"srtb{i}", [128, TC], BF16) for i in range(2)]
    rcb = eka[:, 8192:8192 + NF * TC]
    wbuf = [sb(f"wbuf{i}", [128, 8 * 128], BF16) for i in range(2)]
    wbigv = eka[:, 12288:16384]
    onest = sb("onest", [128, 128], F32)
    vp = sb("vp", [128, 88], F32)
    cv = sb("cv", [128, 4], F32)
    cst = sb("cst", [128, 40], F32)
    strow = ekva[:, :].bitcast(F32)[0:1, 0:4 * T]
    ps = [es.enter_context(nc.psum_tensor(f"ps{i}", [128, TC], F32)) for i in range(8)]

    def hof(ft):
        base = ft * (T + 1)
        return hfull[:, base + 1: base + 1 + T], hfull[:, base: base + T]

    def hzero(ft):
        return hfull[:, ft * (T + 1): ft * (T + 1) + 1]

    mix = lambda ft: mixa[:, ft * T: (ft + 1) * T]
    ek = lambda ft: eka[:, ft * T: (ft + 1) * T]
    ekv = lambda ft: ekva[:, ft * T: (ft + 1) * T]
    mixf32 = mixa[:, :].bitcast(F32)
    Abuf = mixf32[:, 0: T + 1]
    Bbuf = mixf32[:, T + 1: 2 * T + 2]
    DENb = mixf32[:, 2 * T + 2: 3 * T + 2]
    ekf32 = eka[:, :].bitcast(F32)
    xsq = lambda par: ekf32[:, par * T: (par + 1) * T]
    srt = ekf32[:, 0:TC]
    kc = lambda io: ekva[:, io * TC: (io + 1) * TC]

    VI = {k: i for i, k in enumerate(
        ["ln1g", "ln1b", "ln2g", "ln2b", "mk", "mv", "mr", "td", "tf", "mk2", "mr2"])}
    vps = lambda name, ft: vp[:, VI[name] * 8 + ft: VI[name] * 8 + ft + 1]
    alpha_s = lambda ft: cst[:, ft: ft + 1]
    eu_s = lambda ft: cst[:, 8 + ft: 8 + ft + 1]

    st_mu = strow[0:1, 0:T]
    st_tmp = strow[0:1, T:2 * T]
    st_r0 = strow[0:1, 2 * T:3 * T]
    st_r01 = strow[0:1, 3 * T:4 * T]

    for s in ["sIN", "sW", "sPE", "sSC", "sVE", "sGP", "sOUT", "sVV"]:
        p.new_sem(s)

    p.E("sync", lambda e: e.dma_start(out=onest[:], in_=cones), inc="sIN", amt=16)
    p.E("sync", lambda e: e.dma_start(out=vp[:], in_=vpack), inc="sIN", amt=16)
    p.E("sync", lambda e: e.dma_start(out=cv[:], in_=cvals), inc="sIN", amt=16)
    n_in0 = p.n("sIN")
    xld = []
    for ft in range(NF):
        p.E("sync", lambda e, ft=ft: e.dma_start(out=u[ft][:], in_=x0[ft * 128:(ft + 1) * 128, :]),
            inc="sIN", amt=16)
        xld.append(p.n("sIN"))

    p.E("scalar", lambda e: e.activation(out=cst[:, 16:24], in_=vp[:, VI["td"] * 8: VI["td"] * 8 + 8],
                                         func=AF.Exp), waits=[("sIN", n_in0)], inc="sSC")
    p.E("vector", lambda e: e.tensor_scalar_mul(cst[:, 24:32], cst[:, 16:24], -1.0),
        waits=[("sSC", p.n("sSC"))], inc="sVE")
    p.E("scalar", lambda e: e.activation(out=cst[:, 0:8], in_=cst[:, 24:32], func=AF.Exp),
        waits=[("sVE", p.n("sVE"))], inc="sSC")
    p.E("scalar", lambda e: e.activation(out=cst[:, 8:16], in_=vp[:, VI["tf"] * 8: VI["tf"] * 8 + 8],
                                         func=AF.Exp), inc="sSC")

    def emit_stats():
        sq_done = []
        pe_done = []
        for ft in range(NF):
            par = ft % 2
            w = [("sIN", xld[ft])]
            if ft >= 2:
                w.append(("sPE", pe_done[ft - 2]))
            p.E("scalar", lambda e, ft=ft, par=par: e.square(xsq(par), u[ft][:]), waits=w, inc="sSC")
            sq_done.append(p.n("sSC"))
            start = ft == 0
            stop = ft == NF - 1
            p.W("tensor", [("sSC", sq_done[ft]), ("sIN", xld[ft])])
            for j in range(NCH):
                p.E("tensor", lambda e, ft=ft, j=j, start=start, stop=stop:
                    e.matmul(ps[j][0:1, :], onest[:, 0:1], u[ft][:, j * TC:(j + 1) * TC],
                             start=start, stop=stop))
            for j in range(NCH):
                last = j == NCH - 1
                p.E("tensor", lambda e, j=j, par=par, start=start, stop=stop:
                    e.matmul(ps[4 + j][0:1, :], onest[:, 0:1], xsq(par)[:, j * TC:(j + 1) * TC],
                             start=start, stop=stop),
                    inc="sPE" if last else None)
            pe_done.append(p.n("sPE"))
        pe_stats = p.n("sPE")
        for j in range(NCH):
            p.E("vector", lambda e, j=j: e.tensor_scalar_mul(st_mu[:, j * TC:(j + 1) * TC],
                                                             ps[j][0:1, :], 1.0 / D),
                waits=[("sPE", pe_stats)] if j == 0 else (), inc="sVE" if j == NCH - 1 else None)
        ve_mu = p.n("sVE")
        p.E("scalar", lambda e: e.square(st_tmp, st_mu), waits=[("sVE", ve_mu)], inc="sSC")
        sc_m2 = p.n("sSC")
        for j in range(NCH):
            p.E("vector", lambda e, j=j: e.scalar_tensor_tensor(
                out=st_tmp[:, j * TC:(j + 1) * TC], in0=ps[4 + j][0:1, :], scalar=1.0 / D,
                in1=st_tmp[:, j * TC:(j + 1) * TC], op0=ALU.mult, op1=ALU.subtract),
                waits=[("sSC", sc_m2)] if j == 0 else (), inc="sVE" if j == NCH - 1 else None)
        return p.n("sVE")

    def emit_proj(wsrc, rhs_tiles, nout, evac, kin=NF):
        for o in range(nout):
            bid = o % 2
            w = [self_rel for self_rel in [p.wbuf_rel.get(bid)] if self_rel]
            p.E("gpsimd", lambda e, o=o, bid=bid, kin=kin:
                e.dma_start(out=wbuf[bid][:, 0:kin * 128],
                            in_=wsrc[o * 128:(o + 1) * 128, 0:kin * 128]),
                waits=w, inc="sW", amt=16)
            wrdy = p.n("sW")
            for j in range(NCH):
                slot = (o * NCH + j) % 8
                w2 = [("sW", wrdy)]
                if slot in p.slot_rel:
                    w2.append(p.slot_rel[slot])
                p.W("tensor", w2)
                for kt in range(kin):
                    last = kt == kin - 1
                    p.E("tensor", lambda e, kt=kt, j=j, bid=bid, slot=slot:
                        e.matmul(ps[slot][:], wbuf[bid][:, kt * 128:(kt + 1) * 128],
                                 rhs_tiles(kt)[:, j * TC:(j + 1) * TC],
                                 start=(kt == 0), stop=(kt == kin - 1)),
                        inc="sPE" if last else None)
                mark = p.n("sPE")
                if j == NCH - 1:
                    p.wbuf_rel[bid] = ("sPE", mark)
                p.slot_rel[slot] = evac(o, j, ps[slot][:], mark)

    emit_stats()
    p.E("scalar", lambda e: e.activation(out=st_r01, in_=st_tmp, func=AF.Ln, bias=cv[0:1, 0:1]),
        waits=[("sVE", p.n("sVE"))], inc="sSC")
    p.E("scalar", lambda e: e.activation(out=st_r0, in_=st_r01, func=AF.Exp, scale=-0.5), inc="sSC")
    sc_r0 = p.n("sSC")
    p.E("vector", lambda e: e.tensor_mul(st_tmp, st_tmp, st_r0), waits=[("sSC", sc_r0)])
    p.E("vector", lambda e: e.tensor_mul(st_tmp, st_tmp, st_r0), inc="sVE")
    p.E("scalar", lambda e: e.activation(out=st_r01, in_=st_tmp, func=AF.Ln, bias=cv[0:1, 1:2]),
        waits=[("sVE", p.n("sVE"))])
    p.E("scalar", lambda e: e.activation(out=st_r01, in_=st_r01, func=AF.Exp, scale=-0.5), inc="sSC")
    p.E("vector", lambda e: None, waits=[("sSC", p.n("sSC"))], inc="sVE")
    ve_r01 = p.n("sVE")

    p.W("tensor", [("sVE", ve_r01)])
    for j in range(NCH):
        p.E("tensor", lambda e, j=j: e.matmul(ps[j][:], onest[0:1, :], st_mu[:, j * TC:(j + 1) * TC],
                                              start=True, stop=True))
        p.E("tensor", lambda e, j=j: e.matmul(ps[4 + j][:], onest[0:1, :], st_r0[:, j * TC:(j + 1) * TC],
                                              start=True, stop=True), inc="sPE" if j == NCH - 1 else None)
    pe_bc1 = p.n("sPE")
    for ft in range(NF):
        for j in range(NCH):
            w = [("sPE", pe_bc1)] if (ft == 0 and j == 0) else ()
            p.E("vector", lambda e, ft=ft, j=j: e.tensor_tensor(
                u[ft][:, j * TC:(j + 1) * TC], u[ft][:, j * TC:(j + 1) * TC], ps[j][:], ALU.subtract),
                waits=w)
            p.E("vector", lambda e, ft=ft, j=j: e.tensor_tensor(
                u[ft][:, j * TC:(j + 1) * TC], u[ft][:, j * TC:(j + 1) * TC], ps[4 + j][:], ALU.mult),
                inc="sVE" if (ft == NF - 1 and j == NCH - 1) else None)
    ve_u = p.n("sVE")
    p.W("tensor", [("sVE", ve_u)])
    for j in range(NCH):
        p.E("tensor", lambda e, j=j: e.matmul(ps[j][:], onest[0:1, :], st_r01[:, j * TC:(j + 1) * TC],
                                              start=True, stop=True), inc="sPE" if j == NCH - 1 else None)
    pe_bc2 = p.n("sPE")
    for ft in range(NF):
        hc, _ = hof(ft)
        p.E("vector", lambda e, ft=ft: e.memset(hzero(ft), 0.0))
        for j in range(NCH):
            w = [("sPE", pe_bc2)] if (ft == 0 and j == 0) else ()
            p.E("vector", lambda e, ft=ft, j=j, hc=hc: e.tensor_tensor(
                hc[:, j * TC:(j + 1) * TC], u[ft][:, j * TC:(j + 1) * TC], ps[j][:], ALU.mult), waits=w)
        p.E("vector", lambda e, ft=ft, hc=hc: e.tensor_scalar(
            out=hc, in0=hc, scalar1=vps("ln1g", ft), scalar2=vps("ln1b", ft),
            op0=ALU.mult, op1=ALU.add), inc="sVE")
    ve_hln = p.n("sVE")
    for slot in range(8):
        p.slot_rel[slot] = ("sVE", ve_hln)

    def emit_mix(mname):
        done = []
        for ft in range(NF):
            hc, hs = hof(ft)
            p.E("vector", lambda e, hc=hc, hs=hs: e.tensor_tensor(dtmp[:], hc, hs, ALU.subtract))
            p.E("vector", lambda e, ft=ft, hs=hs, mname=mname: e.scalar_tensor_tensor(
                out=mix(ft), in0=dtmp[:], scalar=vps(mname, ft), in1=hs,
                op0=ALU.mult, op1=ALU.add), inc="sVE")
            done.append(p.n("sVE"))
        return done

    p.W("vector", [("sVE", ve_hln)])
    mixk_done = emit_mix("mk")

    def evac_k(o, j, psap, mark):
        p.E("scalar", lambda e, o=o, j=j, psap=psap: e.activation(
            out=ek(o)[:, j * TC:(j + 1) * TC], in_=psap, func=AF.Exp),
            waits=[("sPE", mark)], inc="sSC")
        return ("sSC", p.n("sSC"))

    p.W("tensor", [("sVE", mixk_done[-1])])
    emit_proj(wk, mix, NF, evac_k)
    pe_k_done = p.n("sPE")

    p.W("vector", [("sPE", pe_k_done)])
    mixv_done = emit_mix("mv")
    sc_ek_done = p.n("sSC")

    def evac_v(o, j, psap, mark):
        p.E("vector", lambda e, o=o, j=j, psap=psap: e.tensor_tensor(
            ekv(o)[:, j * TC:(j + 1) * TC], psap, ek(o)[:, j * TC:(j + 1) * TC], ALU.mult),
            waits=[("sPE", mark), ("sSC", sc_ek_done)], inc="sVE")
        return ("sVE", p.n("sVE"))

    p.W("tensor", [("sVE", mixv_done[-1])])
    emit_proj(wv, mix, NF, evac_v)
    pe_v_done = p.n("sPE")
    ve_v_done = p.n("sVE")

    for ft in range(NF):
        p.E("vector", lambda e, ft=ft: e.tensor_scalar(
            out=alre[:], in0=u[0][:], scalar1=0.0, scalar2=alpha_s(ft), op0=ALU.mult, op1=ALU.add),
            waits=[("sVE", ve_v_done), ("sPE", pe_v_done)] if ft == 0 else ())
        p.E("vector", lambda e: e.memset(Abuf[:, 0:1], 0.0))
        p.E("vector", lambda e: e.memset(Bbuf[:, 0:1], 0.0))
        p.E("vector", lambda e, ft=ft: e.tensor_tensor_scan(
            Abuf[:, 1:T + 1], alre[:], ekv(ft), 0.0, ALU.mult, ALU.add))
        p.E("vector", lambda e, ft=ft: e.tensor_tensor_scan(
            Bbuf[:, 1:T + 1], alre[:], ek(ft), 0.0, ALU.mult, ALU.add))
        p.E("vector", lambda e, ft=ft: e.scalar_tensor_tensor(
            out=DENb, in0=ek(ft), scalar=eu_s(ft), in1=Bbuf[:, 0:T], op0=ALU.mult, op1=ALU.add))
        p.E("vector", lambda e: e.reciprocal(DENb, DENb))
        p.E("vector", lambda e, ft=ft: e.scalar_tensor_tensor(
            out=Bbuf[:, 0:T], in0=ekv(ft), scalar=eu_s(ft), in1=Abuf[:, 0:T],
            op0=ALU.mult, op1=ALU.add))
        p.E("vector", lambda e, ft=ft: e.tensor_tensor(ekv(ft), Bbuf[:, 0:T], DENb, ALU.mult),
            inc="sVE")
    ve_scan_done = p.n("sVE")

    p.W("vector", [("sVE", ve_scan_done)])
    mixr_done = emit_mix("mr")

    def evac_r(o, j, psap, mark):
        par = (o * NCH + j) % 2
        p.E("scalar", lambda e, psap=psap, par=par: e.activation(
            out=srtb[par][:], in_=psap, func=AF.Sigmoid),
            waits=[("sPE", mark), ("sVE", p.n("sVE"))], inc="sSC")
        sc_here = p.n("sSC")
        p.E("vector", lambda e, o=o, j=j, par=par: e.tensor_tensor(
            ek(o)[:, j * TC:(j + 1) * TC], srtb[par][:], ekv(o)[:, j * TC:(j + 1) * TC], ALU.mult),
            waits=[("sSC", sc_here)], inc="sVE")
        return ("sSC", sc_here)

    p.W("tensor", [("sVE", mixr_done[-1])])
    emit_proj(wr, mix, NF, evac_r)
    ve_swkv_done = p.n("sVE")

    def evac_o(o, j, psap, mark):
        p.E("vector", lambda e, o=o, j=j, psap=psap: e.tensor_tensor(
            u[o][:, j * TC:(j + 1) * TC], u[o][:, j * TC:(j + 1) * TC], psap, ALU.add),
            waits=[("sPE", mark)], inc="sVE")
        return ("sVE", p.n("sVE"))

    p.W("tensor", [("sVE", ve_swkv_done)])
    emit_proj(wo, ek, NF, evac_o)
    ve_branchA = p.n("sVE")

    p.W("scalar", [("sVE", ve_branchA)])
    p.W("tensor", [("sVE", ve_branchA)])
    emit_stats()
    p.E("scalar", lambda e: e.activation(out=st_tmp, in_=st_tmp, func=AF.Ln, bias=cv[0:1, 1:2]),
        waits=[("sVE", p.n("sVE"))], inc="sSC")
    p.E("scalar", lambda e: e.activation(out=st_r0, in_=st_tmp, func=AF.Exp, scale=-0.5), inc="sSC")
    sc_r2 = p.n("sSC")
    p.W("tensor", [("sSC", sc_r2)])
    for j in range(NCH):
        p.E("tensor", lambda e, j=j: e.matmul(ps[j][:], onest[0:1, :], st_mu[:, j * TC:(j + 1) * TC],
                                              start=True, stop=True))
        p.E("tensor", lambda e, j=j: e.matmul(ps[4 + j][:], onest[0:1, :], st_r0[:, j * TC:(j + 1) * TC],
                                              start=True, stop=True), inc="sPE" if j == NCH - 1 else None)
    pe_bc3 = p.n("sPE")
    for ft in range(NF):
        hc, _ = hof(ft)
        for j in range(NCH):
            w = [("sPE", pe_bc3)] if (ft == 0 and j == 0) else ()
            p.E("vector", lambda e, ft=ft, j=j, hc=hc: e.tensor_tensor(
                hc[:, j * TC:(j + 1) * TC], u[ft][:, j * TC:(j + 1) * TC], ps[j][:], ALU.subtract),
                waits=w)
            p.E("vector", lambda e, j=j, hc=hc: e.tensor_tensor(
                hc[:, j * TC:(j + 1) * TC], hc[:, j * TC:(j + 1) * TC], ps[4 + j][:], ALU.mult))
        p.E("vector", lambda e, ft=ft, hc=hc: e.tensor_scalar(
            out=hc, in0=hc, scalar1=vps("ln2g", ft), scalar2=vps("ln2b", ft),
            op0=ALU.mult, op1=ALU.add), inc="sVE")
    ve_h2 = p.n("sVE")
    for slot in range(8):
        p.slot_rel[slot] = ("sVE", ve_h2)

    xk2 = lambda kt: mixa[:, kt * TC: (kt + 1) * TC]
    xr2 = lambda kt: mixa[:, (NF + kt) * TC: (NF + kt + 1) * TC]
    NIO = 32

    p.W("vector", [("sVE", ve_h2)])
    for n in range(NCH):
        for kt in range(NF):
            hc, hs = hof(kt)
            hcn = hc[:, n * TC:(n + 1) * TC]
            hsn = hs[:, n * TC:(n + 1) * TC]
            w = []
            if n > 0 and kt == 0:
                w = [("sPE", p.aux[("cm_pe", n - 1)])]
            p.E("vector", lambda e, hcn=hcn, hsn=hsn: e.tensor_tensor(
                dtmp[:, 0:TC], hcn, hsn, ALU.subtract), waits=w)
            p.E("vector", lambda e, kt=kt, hsn=hsn: e.scalar_tensor_tensor(
                out=xk2(kt), in0=dtmp[:, 0:TC], scalar=vps("mk2", kt), in1=hsn,
                op0=ALU.mult, op1=ALU.add))
            p.E("vector", lambda e, kt=kt, hsn=hsn: e.scalar_tensor_tensor(
                out=xr2(kt), in0=dtmp[:, 0:TC], scalar=vps("mr2", kt), in1=hsn,
                op0=ALU.mult, op1=ALU.add), inc="sVE")
        ve_mix2 = p.n("sVE")

        for o in range(NF):
            bid = o % 2
            w = [p.wbuf_rel[bid]] if bid in p.wbuf_rel else []
            p.E("gpsimd", lambda e, o=o, bid=bid:
                e.dma_start(out=wbuf[bid][:, 0:NF * 128],
                            in_=cwr[o * 128:(o + 1) * 128, 0:NF * 128]),
                waits=w, inc="sW", amt=16)
            wrdy = p.n("sW")
            slot = o % 8
            w2 = [("sW", wrdy), ("sVE", ve_mix2)]
            if slot in p.slot_rel:
                w2.append(p.slot_rel[slot])
            p.W("tensor", w2)
            for kt in range(NF):
                p.E("tensor", lambda e, kt=kt, bid=bid, slot=slot:
                    e.matmul(ps[slot][:], wbuf[bid][:, kt * 128:(kt + 1) * 128], xr2(kt),
                             start=(kt == 0), stop=(kt == NF - 1)),
                    inc="sPE" if kt == NF - 1 else None)
            p.wbuf_rel[bid] = ("sPE", p.n("sPE"))
            p.E("scalar", lambda e, o=o, slot=slot: e.activation(
                out=rcb[:, o * TC:(o + 1) * TC], in_=ps[slot][:], func=AF.Sigmoid),
                waits=[("sPE", p.n("sPE"))], inc="sSC")
            p.slot_rel[slot] = ("sSC", p.n("sSC"))

        for io in range(NIO):
            bid = io % 2
            w = [p.wbuf_rel[bid]] if bid in p.wbuf_rel else []
            p.E("gpsimd", lambda e, io=io, bid=bid:
                e.dma_start(out=wbuf[bid][:, 0:NF * 128],
                            in_=cwk[io * 128:(io + 1) * 128, 0:NF * 128]),
                waits=w, inc="sW", amt=16)
            wrdy = p.n("sW")
            slot = io % 8
            w2 = [("sW", wrdy)]
            if slot in p.slot_rel:
                w2.append(p.slot_rel[slot])
            p.W("tensor", w2)
            for kt in range(NF):
                p.E("tensor", lambda e, kt=kt, bid=bid, slot=slot:
                    e.matmul(ps[slot][:], wbuf[bid][:, kt * 128:(kt + 1) * 128], xk2(kt),
                             start=(kt == 0), stop=(kt == NF - 1)),
                    inc="sPE" if kt == NF - 1 else None)
            p.wbuf_rel[bid] = ("sPE", p.n("sPE"))
            p.E("scalar", lambda e, io=io, slot=slot: e.activation(
                out=kc(io), in_=ps[slot][:], func=AF.Relu), waits=[("sPE", p.n("sPE"))])
            p.E("scalar", lambda e, io=io: e.square(kc(io), kc(io)), inc="sSC")
            p.slot_rel[slot] = ("sSC", p.n("sSC"))
        sc_kc = p.n("sSC")

        for o in range(NF):
            w = [p.aux["wbig"]] if "wbig" in p.aux else []
            p.E("gpsimd", lambda e, o=o:
                e.dma_start(out=wbigv[:, 0:NIO * 128],
                            in_=cwv[o * 128:(o + 1) * 128, 0:NIO * 128]),
                waits=w, inc="sW", amt=16)
            wrdy = p.n("sW")
            slot = o % 8
            w2 = [("sW", wrdy), ("sSC", sc_kc)]
            if slot in p.slot_rel:
                w2.append(p.slot_rel[slot])
            p.W("tensor", w2)
            for kt in range(NIO):
                p.E("tensor", lambda e, kt=kt, slot=slot:
                    e.matmul(ps[slot][:], wbigv[:, kt * 128:(kt + 1) * 128], kc(kt),
                             start=(kt == 0), stop=(kt == NIO - 1)),
                    inc="sPE" if kt == NIO - 1 else None)
            p.aux["wbig"] = ("sPE", p.n("sPE"))
            p.E("vector", lambda e, o=o, slot=slot: e.tensor_tensor(
                srt[:], ps[slot][:], rcb[:, o * TC:(o + 1) * TC], ALU.mult),
                waits=[("sPE", p.n("sPE"))])
            p.E("vector", lambda e, o=o, n=n: e.tensor_tensor(
                u[o][:, n * TC:(n + 1) * TC], u[o][:, n * TC:(n + 1) * TC], srt[:], ALU.add),
                inc="sVE")
            p.slot_rel[slot] = ("sVE", p.n("sVE"))
        p.aux[("cm_pe", n)] = p.n("sPE")
    ve_done = p.n("sVE")

    for ft in range(NF):
        p.E("gpsimd", lambda e, ft=ft: e.dma_start(out=u[ft][:], in_=x0[ft * 128:(ft + 1) * 128, :],
                                                   accum_op=ALU.add),
            waits=[("sVE", ve_done)] if ft == 0 else (), inc="sGP", amt=16)
    gp_acc = p.n("sGP")
    for ft in range(NF):
        p.E("sync", lambda e, ft=ft: e.dma_start(out=x1[ft * 128:(ft + 1) * 128, :], in_=u[ft][:]),
            waits=[("sGP", gp_acc)] if ft == 0 else (), inc="sOUT", amt=16)
    p.W("sync", [("sOUT", p.n("sOUT"))])

    p.finalize()
    return nc, es


def l0_attn_inputs(core, inp, x0_fm):
    return {
        "x0": x0_fm[core],
        "wk": wprep(inp["L0_attn_tm_Wk"], NF, NF),
        "wv": wprep(inp["L0_attn_tm_Wv"], NF, NF),
        "wr": wprep(inp["L0_attn_tm_Wr"], NF, NF),
        "wo": wprep(inp["L0_attn_Wo"], NF, NF),
        "cwk": wprep(inp["L0_attn_cm_Wk"], NF, 32),
        "cwv": wprep(inp["L0_attn_cm_Wv"], 32, NF),
        "cwr": wprep(inp["L0_attn_cm_Wr"], NF, NF),
        "cones": np.ones((128, 128), np.float32),
        "cvals": np.tile(np.array([1e-5, 1e-3, 0.0, 0.0], np.float32), (128, 1)),
        "vpack": pack_vecs([
            inp["L0_attn_ln1_g"], inp["L0_attn_ln1_b"], inp["L0_attn_ln2_g"], inp["L0_attn_ln2_b"],
            inp["L0_attn_tm_mix_k"], inp["L0_attn_tm_mix_v"], inp["L0_attn_tm_mix_r"],
            inp["L0_attn_time_decay"], inp["L0_attn_time_first"],
            inp["L0_attn_cm_mix_k"], inp["L0_attn_cm_mix_r"],
        ]),
    }


def run_l0_attn(inp):
    x = f32(inp["L0_attn_inputs"])
    x0_fm = [np.ascontiguousarray(x[c].T) for c in range(N_CORES)]
    nc, es = build_l0_attn()
    in_maps = [l0_attn_inputs(c, inp, x0_fm) for c in range(N_CORES)]
    res = run_bass_kernel_spmd(nc, in_maps, core_ids=list(range(N_CORES)))
    es.close()
    return [res.results[c]["x1"] for c in range(N_CORES)]





DEBUG_FFN = 0
DBG_NE = 16
DBG_SHARED = 1

def build_l0_ffn():
    nc = bass.Bass("TRN2", target_bir_lowering=False)
    es = ExitStack()
    p = Prog(nc, es)
    E = 16
    NI = 11
    NS = 22

    x1 = nc.dram_tensor("x1", [D, T], F32, kind="ExternalInput").ap()
    wg = nc.dram_tensor("wg", [E, NI * 128, NF * 128], F32, kind="ExternalInput").ap()
    wu = nc.dram_tensor("wu", [E, NI * 128, NF * 128], F32, kind="ExternalInput").ap()
    wd = nc.dram_tensor("wd", [E, NF * 128, NI * 128], F32, kind="ExternalInput").ap()
    sgw = nc.dram_tensor("sgw", [NS * 128, NF * 128], F32, kind="ExternalInput").ap()
    suw = nc.dram_tensor("suw", [NS * 128, NF * 128], F32, kind="ExternalInput").ap()
    sdw = nc.dram_tensor("sdw", [NF * 128, NS * 128], F32, kind="ExternalInput").ap()
    gwp = nc.dram_tensor("gwp", [128, NF * 16], F32, kind="ExternalInput").ap()
    gbias = nc.dram_tensor("gbias", [128, 16], F32, kind="ExternalInput").ap()
    cones = nc.dram_tensor("cones", [128, 128], F32, kind="ExternalInput").ap()
    ciota = nc.dram_tensor("ciota", [128, 16], F32, kind="ExternalInput").ap()
    cbig = nc.dram_tensor("cbig", [128, 16], F32, kind="ExternalInput").ap()
    cvals2 = nc.dram_tensor("cvals2", [128, 4], F32, kind="ExternalInput").ap()
    coh16 = nc.dram_tensor("coh16", [16, 16 * 128], F32, kind="ExternalInput").ap()
    cident = nc.dram_tensor("cident", [128, 128], F32, kind="ExternalInput").ap()
    x2 = nc.dram_tensor("x2", [D, T], F32, kind="ExternalOutput").ap()
    if DEBUG_FFN:
        dbg_h = nc.dram_tensor("dbg_h", [D, T], F32, kind="ExternalOutput").ap()
        dbg_cw = nc.dram_tensor("dbg_cw", [128, 256], F32, kind="ExternalOutput").ap()
        dbg_sc = nc.dram_tensor("dbg_sc", [128, 256], F32, kind="ExternalOutput").ap()
        dbg_cwb = nc.dram_tensor("dbg_cwb", [128, T], F32, kind="ExternalOutput").ap()
        dbg_cwT = nc.dram_tensor("dbg_cwT", [16, T], F32, kind="ExternalOutput").ap()
        dbg_act = nc.dram_tensor("dbg_act", [128, NI * TC], F32, kind="ExternalOutput").ap()

    sb = lambda name, shape, dt: es.enter_context(nc.sbuf_tensor(name, shape, dt))
    u = [sb(f"u{i}", [128, T], F32) for i in range(NF)]
    h = [sb(f"h{i}", [128, T], BF16) for i in range(NF)]
    bgw = sb("bgw", [128, 2 * NI * NF * 128], BF16)
    bdw = sb("bdw", [128, NF * NI * 128], BF16)
    acta = sb("acta", [128, NI * TC], BF16)
    sgb = [sb(f"sgb{i}", [128, TC], BF16) for i in range(2)]
    rw = sb("rw", [128, NF * 16], BF16)
    cwb = sb("cwb", [128, T], BF16)
    cwT = sb("cwT", [16, T], F32)
    oh16 = sb("oh16", [16, 16 * 128], F32)
    identt = sb("identt", [128, 128], F32)
    onest = sb("onest", [128, 128], F32)
    gw = sb("gw", [128, NF * 16], F32)
    muT = sb("muT", [128, 16], F32)
    rsT = sb("rsT", [128, 16], F32)
    gcb = sb("gcb", [128, 16], F32)
    lgt = sb("lgt", [128, 16], F32)
    s1row = sb("s1row", [1, 16], F32)
    cv = sb("cv", [128, 4], F32)
    gb = sb("gb", [128, 16], F32)
    iot = sb("iot", [128, 16], F32)
    big = sb("big", [128, 16], F32)
    scal = sb("scal", [128, 16 * 16], F32)
    sfc = sb("sfc", [128, 16 * 16], F32)
    gtmp = sb("gtmp", [128, 16 * 6], F32)
    cwall = sb("cwall", [128, 16 * 16], F32)
    wks = sb("wks", [128, 8], F32)
    ohk = sb("ohk", [128, 16], F32)
    ohw = sb("ohw", [128, 16], F32)
    mrow = sb("mrow", [128, 2], F32)
    msk = sb("msk", [128, 16], I32)
    bgwf32 = bgw[:, :].bitcast(F32)
    xsq = [bgwf32[:, i * T:(i + 1) * T] for i in range(2)]
    st_mu = bgwf32[0:1, 2 * T:3 * T]
    st_tmp = bgwf32[0:1, 3 * T:4 * T]
    ps = [es.enter_context(nc.psum_tensor(f"ps{i}", [128, TC], F32)) for i in range(8)]

    for s in ["sIN", "sW", "sPE", "sSC", "sVE", "sGP", "sOUT", "sVV"]:
        p.new_sem(s)

    p.E("sync", lambda e: e.dma_start(out=onest[:], in_=cones), inc="sIN", amt=16)
    p.E("sync", lambda e: e.dma_start(out=gw[:], in_=gwp), inc="sIN", amt=16)
    p.E("sync", lambda e: e.dma_start(out=cv[:], in_=cvals2), inc="sIN", amt=16)
    p.E("gpsimd", lambda e: e.dma_start(out=oh16[:], in_=coh16), inc="sIN", amt=16)
    p.E("sync", lambda e: e.dma_start(out=identt[:], in_=cident), inc="sIN", amt=16)
    p.E("sync", lambda e: e.dma_start(out=gb[:], in_=gbias), inc="sIN", amt=16)
    p.E("sync", lambda e: e.dma_start(out=iot[:], in_=ciota), inc="sIN", amt=16)
    p.E("sync", lambda e: e.dma_start(out=big[:], in_=cbig), inc="sIN", amt=16)
    n_in0 = p.n("sIN")
    xld = []
    for ft in range(NF):
        p.E("sync", lambda e, ft=ft: e.dma_start(out=u[ft][:], in_=x1[ft * 128:(ft + 1) * 128, :]),
            inc="sIN", amt=16)
        xld.append(p.n("sIN"))

    sq_done = []
    pe_done = []
    for ft in range(NF):
        par = ft % 2
        w = [("sIN", xld[ft])]
        if ft >= 2:
            w.append(("sPE", pe_done[ft - 2]))
        p.E("scalar", lambda e, ft=ft, par=par: e.square(xsq[par][:], u[ft][:]), waits=w, inc="sSC")
        sq_done.append(p.n("sSC"))
        start = ft == 0
        stop = ft == NF - 1
        p.W("tensor", [("sSC", sq_done[ft]), ("sIN", xld[ft])])
        for j in range(NCH):
            p.E("tensor", lambda e, ft=ft, j=j, start=start, stop=stop:
                e.matmul(ps[j][0:1, :], onest[:, 0:1], u[ft][:, j * TC:(j + 1) * TC],
                         start=start, stop=stop))
        for j in range(NCH):
            p.E("tensor", lambda e, j=j, par=par, start=start, stop=stop:
                e.matmul(ps[4 + j][0:1, :], onest[:, 0:1], xsq[par][:, j * TC:(j + 1) * TC],
                         start=start, stop=stop),
                inc="sPE" if j == NCH - 1 else None)
        pe_done.append(p.n("sPE"))
    pe_stats = p.n("sPE")
    for j in range(NCH):
        p.E("vector", lambda e, j=j: e.tensor_scalar_mul(st_mu[:, j * TC:(j + 1) * TC],
                                                         ps[j][0:1, :], 1.0 / D),
            waits=[("sPE", pe_stats)] if j == 0 else (), inc="sVE" if j == NCH - 1 else None)
    p.E("scalar", lambda e: e.square(st_tmp, st_mu), waits=[("sVE", p.n("sVE"))], inc="sSC")
    sc_m2 = p.n("sSC")
    for j in range(NCH):
        p.E("vector", lambda e, j=j: e.scalar_tensor_tensor(
            out=st_tmp[:, j * TC:(j + 1) * TC], in0=ps[4 + j][0:1, :], scalar=1.0 / D,
            in1=st_tmp[:, j * TC:(j + 1) * TC], op0=ALU.mult, op1=ALU.subtract),
            waits=[("sSC", sc_m2)] if j == 0 else (), inc="sVE" if j == NCH - 1 else None)
    p.E("scalar", lambda e: e.activation(out=st_tmp, in_=st_tmp, func=AF.Ln, bias=cv[0:1, 0:1]),
        waits=[("sVE", p.n("sVE"))], inc="sSC")
    p.E("scalar", lambda e: e.activation(out=st_tmp, in_=st_tmp, func=AF.Exp, scale=-0.5), inc="sSC")
    sc_rstd = p.n("sSC")
    p.W("tensor", [("sSC", sc_rstd), ("sIN", n_in0)])
    for tb in range(16):
        sl = 6 + tb % 2
        if ("gmr", sl) in p.aux:
            p.W("tensor", [p.aux[("gmr", sl)]])
        p.E("tensor", lambda e, tb=tb, sl=sl: e.transpose(
            ps[sl][0:128, 0:1], st_mu[0:1, tb * 128:(tb + 1) * 128], identt[0:1, 0:1]))
        p.E("tensor", lambda e, tb=tb, sl=sl: e.transpose(
            ps[sl][0:128, 1:2], st_tmp[0:1, tb * 128:(tb + 1) * 128], identt[0:1, 0:1]), inc="sPE")
        pe_t = p.n("sPE")
        p.EV("vector", lambda e, tb=tb, sl=sl: e.tensor_copy(muT[:, tb:tb + 1], ps[sl][0:128, 0:1]),
             waits=[("sPE", pe_t)])
        p.EV("vector", lambda e, tb=tb, sl=sl: e.tensor_copy(rsT[:, tb:tb + 1], ps[sl][0:128, 1:2]))
        p.aux[("gmr", sl)] = ("sVV", p.n("sVV"))
    p.W("tensor", [p.aux[("gmr", 6)]])
    for kt in range(NF):
        p.E("tensor", lambda e, kt=kt: e.matmul(ps[6][0:1, 0:16], onest[:, 0:1],
                                                gw[:, kt * 16:(kt + 1) * 16],
                                                start=(kt == 0), stop=(kt == NF - 1)),
            inc="sPE" if kt == NF - 1 else None)
    pe_s1g = p.n("sPE")
    p.EV("vector", lambda e: e.tensor_copy(s1row[0:1, 0:16], ps[6][0:1, 0:16]),
         waits=[("sPE", pe_s1g)])
    vv_s1r = p.n("sVV")
    p.W("tensor", [("sVV", vv_s1r)])
    p.E("tensor", lambda e: e.matmul(ps[6][:, 0:16], onest[0:1, :], s1row[0:1, 0:16],
                                     start=True, stop=True), inc="sPE")
    pe_s1b = p.n("sPE")
    p.EV("vector", lambda e: e.tensor_copy(gcb[:], ps[6][:, 0:16]), waits=[("sPE", pe_s1b)])
    vv_gcb = p.n("sVV")
    p.W("tensor", [("sSC", sc_rstd), ("sVV", vv_gcb)])
    for j in range(NCH):
        p.E("tensor", lambda e, j=j: e.matmul(ps[j][:], onest[0:1, :], st_mu[:, j * TC:(j + 1) * TC],
                                              start=True, stop=True))
        p.E("tensor", lambda e, j=j: e.matmul(ps[4 + j][:], onest[0:1, :], st_tmp[:, j * TC:(j + 1) * TC],
                                              start=True, stop=True), inc="sPE" if j == NCH - 1 else None)
    pe_bc = p.n("sPE")
    for ft in range(NF):
        for j in range(NCH):
            w = [("sPE", pe_bc)] if (ft == 0 and j == 0) else ()
            p.E("vector", lambda e, ft=ft, j=j: e.tensor_tensor(
                h[ft][:, j * TC:(j + 1) * TC], u[ft][:, j * TC:(j + 1) * TC], ps[j][:], ALU.subtract),
                waits=w)
            p.E("vector", lambda e, ft=ft, j=j: e.tensor_tensor(
                h[ft][:, j * TC:(j + 1) * TC], h[ft][:, j * TC:(j + 1) * TC], ps[4 + j][:], ALU.mult),
                inc="sVE" if (ft == NF - 1 and j == NCH - 1) else None)
    ve_h = p.n("sVE")
    for slot in range(8):
        p.slot_rel[slot] = ("sVE", ve_h)

    p.W("tensor", [("sVE", ve_h)])
    for tb in range(16):
        slot = tb % 2
        if slot in p.slot_rel:
            p.W("tensor", [p.slot_rel[slot]])
        for kt in range(NF):
            p.E("tensor", lambda e, tb=tb, kt=kt, slot=slot:
                e.matmul(ps[slot][:, 0:16], u[kt][:, tb * 128:(tb + 1) * 128],
                         gw[:, kt * 16:(kt + 1) * 16], start=(kt == 0), stop=(kt == NF - 1)),
                inc="sPE" if kt == NF - 1 else None)
        pe_l = p.n("sPE")
        sc_prev = p.n("sSC")
        p.EV("vector", lambda e, tb=tb: e.tensor_scalar(
            out=ohw[:], in0=gcb[:], scalar1=muT[:, tb:tb + 1], scalar2=None, op0=ALU.mult),
            waits=[("sPE", pe_l), ("sSC", sc_prev)])
        p.EV("vector", lambda e, slot=slot: e.tensor_tensor(
            lgt[:], ps[slot][:, 0:16], ohw[:], ALU.subtract))
        p.EV("vector", lambda e, tb=tb: e.tensor_scalar(
            out=lgt[:], in0=lgt[:], scalar1=rsT[:, tb:tb + 1], scalar2=None, op0=ALU.mult))
        vv_lg = p.n("sVV")
        p.slot_rel[slot] = ("sVV", vv_lg)
        p.E("scalar", lambda e, tb=tb: e.activation(
            out=scal[:, tb * 16:(tb + 1) * 16], in_=lgt[:], func=AF.Sigmoid),
            waits=[("sVV", vv_lg)], inc="sSC")
    sc_scores = p.n("sSC")
    p.W("vector", [("sSC", sc_scores)])
    for tb in range(16):
        p.EV("vector", lambda e, tb=tb: e.tensor_tensor(
            sfc[:, tb * 16:(tb + 1) * 16], scal[:, tb * 16:(tb + 1) * 16], gb[:], ALU.add))
    def sv(base, e):
        return base[:, e::16]

    def gt(i):
        return gtmp[:, i * 16:(i + 1) * 16]
    for g in range(4):
        c = 4 * g
        p.EV("vector", lambda e, c=c: e.tensor_tensor(gt(0), sv(sfc, c), sv(sfc, c + 1), ALU.max))
        p.EV("vector", lambda e, c=c: e.tensor_tensor(gt(1), sv(sfc, c + 2), sv(sfc, c + 3), ALU.max))
        p.EV("vector", lambda e, c=c: e.tensor_tensor(gt(2), sv(sfc, c), sv(sfc, c + 1), ALU.min))
        p.EV("vector", lambda e, c=c: e.tensor_tensor(gt(3), sv(sfc, c + 2), sv(sfc, c + 3), ALU.min))
        p.EV("vector", lambda e: e.tensor_tensor(gt(4), gt(0), gt(1), ALU.max))
        p.EV("vector", lambda e: e.tensor_tensor(gt(5), gt(0), gt(1), ALU.min))
        p.EV("vector", lambda e: e.tensor_tensor(gt(3), gt(2), gt(3), ALU.max))
        p.EV("vector", lambda e: e.tensor_tensor(gt(5), gt(5), gt(3), ALU.max))
        p.EV("vector", lambda e, g=g: e.tensor_tensor(sv(cwall, g), gt(4), gt(5), ALU.add))
    p.EV("vector", lambda e: e.tensor_tensor(gt(0), sv(cwall, 0), sv(cwall, 1), ALU.max))
    p.EV("vector", lambda e: e.tensor_tensor(gt(1), sv(cwall, 2), sv(cwall, 3), ALU.max))
    p.EV("vector", lambda e: e.tensor_tensor(gt(2), sv(cwall, 0), sv(cwall, 1), ALU.min))
    p.EV("vector", lambda e: e.tensor_tensor(gt(3), sv(cwall, 2), sv(cwall, 3), ALU.min))
    p.EV("vector", lambda e: e.tensor_tensor(gt(5), gt(0), gt(1), ALU.min))
    p.EV("vector", lambda e: e.tensor_tensor(gt(3), gt(2), gt(3), ALU.max))
    p.EV("vector", lambda e: e.tensor_tensor(gt(5), gt(5), gt(3), ALU.max))
    for g in range(4):
        p.EV("vector", lambda e, g=g: e.tensor_tensor(gt(2), sv(cwall, g), gt(5), ALU.is_ge))
        for i in range(4):
            ecol = 4 * g + i
            p.EV("vector", lambda e, ecol=ecol: e.tensor_tensor(
                sv(sfc, ecol), sv(sfc, ecol), gt(2), ALU.mult))
    ve_gate1 = p.n("sVE")
    for tb in range(16):
        sfc_tb = sfc[:, tb * 16:(tb + 1) * 16]
        scal_tb = scal[:, tb * 16:(tb + 1) * 16]
        cw_tb = cwall[:, tb * 16:(tb + 1) * 16]
        p.EV("vector", lambda e, cw_tb=cw_tb: e.memset(cw_tb, 0.0))
        for k in range(4):
            p.EV("vector", lambda e, sfc_tb=sfc_tb: e.reduce_max(mrow[:, 0:1], sfc_tb, axis=AX.X))
            p.EV("vector", lambda e, sfc_tb=sfc_tb: e.tensor_scalar(
                out=msk[:], in0=sfc_tb, scalar1=mrow[:, 0:1], scalar2=None, op0=ALU.is_ge))
            p.EV("vector", lambda e: e.select(out=ohk[:], mask=msk[:], on_true=iot[:], on_false=big[:]))
            p.EV("vector", lambda e: e.tensor_reduce(mrow[:, 1:2], ohk[:], axis=AX.X, op=ALU.min))
            p.EV("vector", lambda e: e.tensor_scalar(
                out=ohk[:], in0=iot[:], scalar1=mrow[:, 1:2], scalar2=None, op0=ALU.is_equal))
            p.EV("vector", lambda e, scal_tb=scal_tb: e.tensor_tensor(
                ohw[:], ohk[:], scal_tb, ALU.mult))
            p.EV("vector", lambda e, k=k: e.reduce_sum(wks[:, k:k + 1], ohw[:], axis=AX.X))
            p.EV("vector", lambda e, sfc_tb=sfc_tb: e.scalar_tensor_tensor(
                out=sfc_tb, in0=ohk[:], scalar=-1e30, in1=sfc_tb, op0=ALU.mult, op1=ALU.add))
            p.EV("vector", lambda e, cw_tb=cw_tb, k=k: e.scalar_tensor_tensor(
                out=cw_tb, in0=ohk[:], scalar=wks[:, k:k + 1], in1=cw_tb,
                op0=ALU.mult, op1=ALU.add))
        p.EV("vector", lambda e: e.reduce_sum(wks[:, 4:5], wks[:, 0:4], axis=AX.X))
        p.EV("vector", lambda e: e.tensor_scalar_add(wks[:, 4:5], wks[:, 4:5], 1e-20))
        p.EV("vector", lambda e: e.reciprocal(wks[:, 5:6], wks[:, 4:5]))
        p.EV("vector", lambda e, cw_tb=cw_tb: e.tensor_scalar(
            out=cw_tb, in0=cw_tb, scalar1=wks[:, 5:6], scalar2=None, op0=ALU.mult))
        p.E("vector", lambda e: None, inc="sVE")
    ve_gate = p.n("sVE")
    if DEBUG_FFN:
        for ft in range(NF):
            p.E("scalar", lambda e, ft=ft: e.copy(xsq[0][:], h[ft][:]),
                waits=[("sVE", ve_gate)], inc="sSC")
            p.E("gpsimd", lambda e, ft=ft: e.dma_start(out=dbg_h[ft * 128:(ft + 1) * 128, :], in_=xsq[0][:]),
                waits=[("sSC", p.n("sSC"))], inc="sGP", amt=16)
            p.W("scalar", [("sGP", p.n("sGP"))])
        p.E("gpsimd", lambda e: e.dma_start(out=dbg_cw, in_=cwall[:]), inc="sGP", amt=16)
        p.E("gpsimd", lambda e: e.dma_start(out=dbg_sc, in_=scal[:]), inc="sGP", amt=16)

    def load_gu(e_, wait=None):
        for it in range(NI):
            p.E("gpsimd", lambda e, e_=e_, it=it: e.dma_start(
                out=bgw[:, it * NF * 128:(it + 1) * NF * 128],
                in_=wg[e_, it * 128:(it + 1) * 128, :]),
                waits=[wait] if (wait and it == 0) else (), inc="sW", amt=16)
        for it in range(NI):
            p.E("gpsimd", lambda e, e_=e_, it=it: e.dma_start(
                out=bgw[:, (NI + it) * NF * 128:(NI + it + 1) * NF * 128],
                in_=wu[e_, it * 128:(it + 1) * 128, :]),
                inc="sW", amt=16)
        return p.n("sW")

    def load_d(e_, wait=None):
        for o in range(NF):
            p.E("gpsimd", lambda e, e_=e_, o=o: e.dma_start(
                out=bdw[:, o * NI * 128:(o + 1) * NI * 128],
                in_=wd[e_, o * 128:(o + 1) * 128, :]),
                waits=[wait] if (wait and o == 0) else (), inc="sW", amt=16)
        return p.n("sW")

    wrdy_gu = load_gu(0, wait=("sVE", ve_h))
    wrdy_d = load_d(0, wait=("sVE", ve_h))

    w0 = [("sVE", ve_gate)]
    for sl in (6, 7):
        if sl in p.slot_rel:
            w0 = w0 + [p.slot_rel[sl]]
    p.W("tensor", w0)
    for tb in range(16):
        sl = 6 + tb % 2
        if tb >= 2:
            p.W("tensor", [p.aux[("cwt", tb - 2)]])
        p.E("tensor", lambda e, tb=tb, sl=sl: e.transpose(
            ps[sl][0:16, 0:128], cwall[:, tb * 16:(tb + 1) * 16], identt[:]), inc="sPE")
        pe_tr = p.n("sPE")
        p.E("vector", lambda e, tb=tb, sl=sl: e.tensor_copy(
            cwT[:, tb * 128:(tb + 1) * 128], ps[sl][0:16, 0:128]),
            waits=[("sPE", pe_tr)], inc="sVE")
        p.aux[("cwt", tb)] = ("sVE", p.n("sVE"))
    ve_cwT = p.n("sVE")
    p.slot_rel[6] = ("sVE", ve_cwT)
    p.slot_rel[7] = ("sVE", ve_cwT)
    if DEBUG_FFN:
        p.E("gpsimd", lambda e: e.dma_start(out=dbg_cwT, in_=cwT[:]),
            waits=[("sVE", ve_cwT)], inc="sGP", amt=16)

    for e_ in range(DBG_NE):
        p.W("tensor", [("sVE", ve_cwT)] if e_ == 0 else [])
        for j in range(NCH):
            sl = 6 + j % 2
            if ("cwb", sl) in p.aux:
                p.W("tensor", [p.aux[("cwb", sl)]])
            p.E("tensor", lambda e, j=j, e_=e_, sl=sl: e.matmul(
                ps[sl][:], oh16[:, e_ * 128:(e_ + 1) * 128],
                cwT[0:16, j * TC:(j + 1) * TC], start=True, stop=True), inc="sPE")
            pe_b = p.n("sPE")
            p.E("vector", lambda e, j=j, sl=sl: e.tensor_copy(cwb[:, j * TC:(j + 1) * TC],
                                                              ps[sl][:]),
                waits=[("sPE", pe_b)], inc="sVE")
            p.aux[("cwb", sl)] = ("sVE", p.n("sVE"))
        ve_cwb = p.n("sVE")

        if DEBUG_FFN and e_ == 0:
            p.E("gpsimd", lambda e: e.dma_start(out=dbg_cwb, in_=cwb[:]),
                waits=[("sVE", ve_cwb)], inc="sGP", amt=16)
            p.W("gpsimd", [("sGP", p.n("sGP"))])
        p.W("tensor", [("sW", wrdy_gu)])
        for n in range(NCH):
            for it in range(NI):
                sg_ = (n * NI + it) % 2
                slot_g = sg_ * 2
                slot_u = sg_ * 2 + 1
                wmm = []
                if slot_g in p.slot_rel:
                    wmm.append(p.slot_rel[slot_g])
                if slot_u in p.slot_rel:
                    wmm.append(p.slot_rel[slot_u])
                if wmm:
                    p.W("tensor", wmm)
                for kt in range(NF):
                    p.E("tensor", lambda e, it=it, kt=kt, n=n, slot_g=slot_g:
                        e.matmul(ps[slot_g][:], bgw[:, (it * NF + kt) * 128:(it * NF + kt + 1) * 128],
                                 h[kt][:, n * TC:(n + 1) * TC], start=(kt == 0), stop=(kt == NF - 1)),
                        inc="sPE" if kt == NF - 1 else None)
                pe_g = p.n("sPE")
                for kt in range(NF):
                    p.E("tensor", lambda e, it=it, kt=kt, n=n, slot_u=slot_u:
                        e.matmul(ps[slot_u][:], bgw[:, ((NI + it) * NF + kt) * 128:((NI + it) * NF + kt + 1) * 128],
                                 h[kt][:, n * TC:(n + 1) * TC], start=(kt == 0), stop=(kt == NF - 1)),
                        inc="sPE" if kt == NF - 1 else None)
                pe_u = p.n("sPE")
                if n == NCH - 1 and it == NI - 1:
                    p.aux["gu_done_cur"] = pe_u
                par = (n * NI + it) % 2
                p.E("scalar", lambda e, par=par, slot_g=slot_g: e.activation(
                    out=sgb[par][:], in_=ps[slot_g][:], func=AF.Silu),
                    waits=[("sPE", pe_g), ("sVE", p.n("sVE"))], inc="sSC")
                sc_sg = p.n("sSC")
                p.slot_rel[slot_g] = ("sSC", sc_sg)
                wv = [("sSC", sc_sg), ("sPE", pe_u)]
                if "dn_mark" in p.aux:
                    wv.append(("sPE", p.aux["dn_mark"]))
                p.E("vector", lambda e, it=it, par=par, slot_u=slot_u: e.tensor_tensor(
                    acta[:, it * TC:(it + 1) * TC], ps[slot_u][:], sgb[par][:], ALU.mult), waits=wv)
                p.E("vector", lambda e, it=it, n=n: e.scalar_tensor_tensor(
                    out=acta[:, it * TC:(it + 1) * TC], in0=acta[:, it * TC:(it + 1) * TC],
                    scalar=0.0, in1=acta[:, it * TC:(it + 1) * TC],
                    op0=ALU.mult, op1=ALU.add) if False else e.tensor_tensor(
                    acta[:, it * TC:(it + 1) * TC], acta[:, it * TC:(it + 1) * TC],
                    cwb[:, n * TC:(n + 1) * TC], ALU.mult), inc="sVE")
                p.slot_rel[slot_u] = ("sVE", p.n("sVE"))
            ve_act = p.n("sVE")
            if n == 0 and e_ < E - 1:
                pass
            p.W("tensor", [("sW", wrdy_d), ("sVE", ve_act)])
            for o in range(NF):
                slot = 4 + o % 2
                if slot in p.slot_rel:
                    p.W("tensor", [p.slot_rel[slot]])
                for kt in range(NI):
                    p.E("tensor", lambda e, o=o, kt=kt, slot=slot:
                        e.matmul(ps[slot][:], bdw[:, (o * NI + kt) * 128:(o * NI + kt + 1) * 128],
                                 acta[:, kt * TC:(kt + 1) * TC], start=(kt == 0), stop=(kt == NI - 1)),
                        inc="sPE" if kt == NI - 1 else None)
                pe_d = p.n("sPE")
                p.E("vector", lambda e, o=o, n=n, slot=slot: e.tensor_tensor(
                    u[o][:, n * TC:(n + 1) * TC], u[o][:, n * TC:(n + 1) * TC], ps[slot][:], ALU.add),
                    waits=[("sPE", pe_d)], inc="sVE")
                p.slot_rel[slot] = ("sVE", p.n("sVE"))
            p.aux["dn_mark"] = p.n("sPE")
            if DEBUG_FFN and e_ == 0 and n == 0:
                p.E("gpsimd", lambda e: e.dma_start(out=dbg_act, in_=acta[:]),
                    waits=[("sVE", p.n("sVE"))], inc="sGP", amt=16)
                p.W("gpsimd", [("sGP", p.n("sGP"))])
        if e_ < DBG_NE - 1:
            wrdy_gu = load_gu(e_ + 1, wait=("sPE", p.aux["gu_done_cur"]))
            wrdy_d = load_d(e_ + 1, wait=("sPE", p.n("sPE")))
        p.aux["gu_done"] = p.n("sPE")

    acts = bgw[:, 0:NS * TC]
    if not DBG_SHARED:
        ve_done = p.n("sVE")
        for ft in range(NF):
            p.E("sync", lambda e, ft=ft: e.dma_start(out=x2[ft * 128:(ft + 1) * 128, :], in_=u[ft][:]),
                waits=[("sVE", ve_done)] if ft == 0 else (), inc="sOUT", amt=16)
        p.W("sync", [("sOUT", p.n("sOUT"))])
        p.finalize()
        return nc, es
    sg_w = [("sPE", p.aux["gu_done"])]
    sgw_t = lambda b: bdw[:, b * NF * 128:(b + 1) * NF * 128]

    for n in range(NCH):
        for io in range(NS):
            b = io % 2
            bu = 2 + io % 2
            wdep = [p.aux[("shw", b)]] if ("shw", b) in p.aux else (sg_w if io < 2 and n == 0 else [])
            p.E("gpsimd", lambda e, io=io, b=b: e.dma_start(
                out=sgw_t(b), in_=sgw[io * 128:(io + 1) * 128, :]), waits=wdep, inc="sW", amt=16)
            wdep2 = [p.aux[("shw", bu)]] if ("shw", bu) in p.aux else []
            p.E("gpsimd", lambda e, io=io, bu=bu: e.dma_start(
                out=sgw_t(bu), in_=suw[io * 128:(io + 1) * 128, :]), waits=wdep2, inc="sW", amt=16)
            wrdy = p.n("sW")
            slot_g = io % 2
            slot_u = 2 + io % 2
            wmm = [("sW", wrdy)]
            for sl in (slot_g, slot_u):
                if sl in p.slot_rel:
                    wmm.append(p.slot_rel[sl])
            p.W("tensor", wmm)
            for kt in range(NF):
                p.E("tensor", lambda e, kt=kt, n=n, b=b, slot_g=slot_g:
                    e.matmul(ps[slot_g][:], sgw_t(b)[:, kt * 128:(kt + 1) * 128],
                             h[kt][:, n * TC:(n + 1) * TC], start=(kt == 0), stop=(kt == NF - 1)),
                    inc="sPE" if kt == NF - 1 else None)
            pe_g = p.n("sPE")
            p.aux[("shw", b)] = ("sPE", pe_g)
            for kt in range(NF):
                p.E("tensor", lambda e, kt=kt, n=n, bu=bu, slot_u=slot_u:
                    e.matmul(ps[slot_u][:], sgw_t(bu)[:, kt * 128:(kt + 1) * 128],
                             h[kt][:, n * TC:(n + 1) * TC], start=(kt == 0), stop=(kt == NF - 1)),
                    inc="sPE" if kt == NF - 1 else None)
            pe_u = p.n("sPE")
            p.aux[("shw", bu)] = ("sPE", pe_u)
            par = io % 2
            p.E("scalar", lambda e, par=par, slot_g=slot_g: e.activation(
                out=sgb[par][:], in_=ps[slot_g][:], func=AF.Silu),
                waits=[("sPE", pe_g), ("sVE", p.n("sVE"))], inc="sSC")
            sc_sg = p.n("sSC")
            p.slot_rel[slot_g] = ("sSC", sc_sg)
            p.E("vector", lambda e, io=io, par=par, slot_u=slot_u: e.tensor_tensor(
                acts[:, io * TC:(io + 1) * TC], ps[slot_u][:], sgb[par][:], ALU.mult),
                waits=[("sSC", sc_sg), ("sPE", pe_u)], inc="sVE")
            p.slot_rel[slot_u] = ("sVE", p.n("sVE"))
        ve_acts = p.n("sVE")
        p.W("tensor", [("sVE", ve_acts)])
        for o in range(NF):
            b = 4 + o % 2
            off = (4 * NF * 128) + (o % 2) * NS * 128
            wdep = [p.aux[("shd", o % 2)]] if ("shd", o % 2) in p.aux else []
            p.E("gpsimd", lambda e, o=o, off=off: e.dma_start(
                out=bdw[:, off:off + NS * 128], in_=sdw[o * 128:(o + 1) * 128, :]),
                waits=wdep, inc="sW", amt=16)
            wrdy = p.n("sW")
            slot = 4 + o % 2
            wmm = [("sW", wrdy)]
            if slot in p.slot_rel:
                wmm.append(p.slot_rel[slot])
            p.W("tensor", wmm)
            for kt in range(NS):
                p.E("tensor", lambda e, o=o, kt=kt, off=off, slot=slot:
                    e.matmul(ps[slot][:], bdw[:, off + kt * 128: off + (kt + 1) * 128],
                             acts[:, kt * TC:(kt + 1) * TC], start=(kt == 0), stop=(kt == NS - 1)),
                    inc="sPE" if kt == NS - 1 else None)
            pe_d = p.n("sPE")
            p.aux[("shd", o % 2)] = ("sPE", pe_d)
            p.E("vector", lambda e, o=o, n=n, slot=slot: e.tensor_tensor(
                u[o][:, n * TC:(n + 1) * TC], u[o][:, n * TC:(n + 1) * TC], ps[slot][:], ALU.add),
                waits=[("sPE", pe_d)], inc="sVE")
            p.slot_rel[slot] = ("sVE", p.n("sVE"))
    ve_done = p.n("sVE")

    for ft in range(NF):
        p.E("sync", lambda e, ft=ft: e.dma_start(out=x2[ft * 128:(ft + 1) * 128, :], in_=u[ft][:]),
            waits=[("sVE", ve_done)] if ft == 0 else (), inc="sOUT", amt=16)
    p.W("sync", [("sOUT", p.n("sOUT"))])

    p.finalize()
    return nc, es


def l0_ffn_inputs(core, inp, x1_fm):
    E, NI, NS = 16, 11, 22
    wg = np.stack([wprep(inp["L0_ffn_gate_proj"][e].T, NF, NI) for e in range(E)])
    wu = np.stack([wprep(inp["L0_ffn_up_proj"][e].T, NF, NI) for e in range(E)])
    wd = np.stack([wprep(inp["L0_ffn_down_proj"][e].T, NI, NF) for e in range(E)])
    gwp = np.ascontiguousarray(
        f32(inp["L0_ffn_gate_w"]).T.reshape(NF, 128, 16).transpose(1, 0, 2).reshape(128, NF * 16))
    iota16 = np.tile(np.arange(16, dtype=np.float32), (128, 1))
    return {
        "x1": x1_fm[core],
        "wg": wg, "wu": wu, "wd": wd,
        "sgw": wprep(f32(inp["L0_ffn_shared_gate_w"]).T, NF, NS),
        "suw": wprep(f32(inp["L0_ffn_shared_up_w"]).T, NF, NS),
        "sdw": wprep(f32(inp["L0_ffn_shared_down_w"]).T, NS, NF),
        "gwp": gwp,
        "gbias": np.tile(f32(inp["L0_ffn_gate_bias"]), (128, 1)),
        "cones": np.ones((128, 128), np.float32),
        "ciota": iota16,
        "cbig": np.full((128, 16), 1e30, np.float32),
        "cvals2": np.tile(np.array([1e-5, 1e-3, 0.0, 0.0], np.float32), (128, 1)),
        "cident": np.eye(128, dtype=np.float32),
        "coh16": np.concatenate([np.tile((np.arange(16) == e).astype(np.float32)[:, None], (1, 128))
                                 for e in range(16)], axis=1),
    }


def run_l0_ffn(inp, x1_fm):
    nc, es = build_l0_ffn()
    in_maps = [l0_ffn_inputs(c, inp, x1_fm) for c in range(N_CORES)]
    res = run_bass_kernel_spmd(nc, in_maps, core_ids=list(range(N_CORES)))
    es.close()
    if DEBUG_FFN:
        return res
    return [res.results[c]["x2"] for c in range(N_CORES)]





DEBUG_L1A = 0

def build_l1_attn():
    nc = bass.Bass("TRN2", target_bir_lowering=False)
    es = ExitStack()
    p = Prog(nc, es)
    NH = 16
    HD = 64
    DQ = 512
    NQ = DQ // 128
    SCL = float(np.sqrt(np.float32(16 + 512 + 512)))

    x2 = nc.dram_tensor("x2", [D, T], F32, kind="ExternalInput").ap()
    wdq = nc.dram_tensor("wdq", [NQ * 128, NF * 128], F32, kind="ExternalInput").ap()
    wuq = nc.dram_tensor("wuq", [NF * 128, NQ * 128], F32, kind="ExternalInput").ap()
    wqr = nc.dram_tensor("wqr", [NF * 128, NQ * 128], F32, kind="ExternalInput").ap()
    wqrs = nc.dram_tensor("wqrs", [NF * 128, NQ * 128], F32, kind="ExternalInput").ap()
    wkr = nc.dram_tensor("wkr", [NF * 128, NF * 128], F32, kind="ExternalInput").ap()
    wkrs = nc.dram_tensor("wkrs", [NF * 128, NF * 128], F32, kind="ExternalInput").ap()
    wdkv = nc.dram_tensor("wdkv", [NQ * 128, NF * 128], F32, kind="ExternalInput").ap()
    wuk = nc.dram_tensor("wuk", [NF * 128, NQ * 128], F32, kind="ExternalInput").ap()
    wuv = nc.dram_tensor("wuv", [DQ, D], F32, kind="ExternalInput").ap()
    wo = nc.dram_tensor("wo", [NF * 128, NF * 128], F32, kind="ExternalInput").ap()
    ctab = nc.dram_tensor("ctab", [D, T], F32, kind="ExternalInput").ap()
    stab = nc.dram_tensor("stab", [D, T], F32, kind="ExternalInput").ap()
    vpack = nc.dram_tensor("vpack", [128, 80], F32, kind="ExternalInput").ap()
    vbias_d = nc.dram_tensor("vbias_d", [128, D], F32, kind="ExternalInput").ap()
    cones = nc.dram_tensor("cones", [128, 128], F32, kind="ExternalInput").ap()
    cvals = nc.dram_tensor("cvals", [128, 4], F32, kind="ExternalInput").ap()
    x3 = nc.dram_tensor("x3", [D, T], F32, kind="ExternalOutput").ap()
    if DEBUG_L1A:
        dbg_h = nc.dram_tensor("dbg_h", [D, T], F32, kind="ExternalOutput").ap()
        dbg_ctq = nc.dram_tensor("dbg_ctq", [DQ, T], F32, kind="ExternalOutput").ap()
        dbg_q = nc.dram_tensor("dbg_q", [D, T], F32, kind="ExternalOutput").ap()
        dbg_k = nc.dram_tensor("dbg_k", [D, T], F32, kind="ExternalOutput").ap()
        dbg_v = nc.dram_tensor("dbg_v", [128, 16 * NH * 65], F32, kind="ExternalOutput").ap()
        dbg_ao = nc.dram_tensor("dbg_ao", [D, T], F32, kind="ExternalOutput").ap()

    sb = lambda name, shape, dt: es.enter_context(nc.sbuf_tensor(name, shape, dt))
    h = [sb(f"h{i}", [128, T], BF16) for i in range(NF)]
    query = sb("query", [128, NF * T], BF16)
    key = sb("key", [128, NF * T], BF16)
    vtm = sb("vtm", [128, 16 * (NH * 65 + 0)], BF16)
    ctq = sb("ctq", [128, NQ * T], BF16)
    tabc = sb("tabc", [128, T], BF16)
    tabs = sb("tabs", [128, T], BF16)
    t32a = sb("t32a", [128, T], F32)
    t32b = sb("t32b", [128, T], F32)
    xio = [t32a[:, i * TC:(i + 1) * TC] for i in range(2)]
    xst = [t32b[:, i * TC:(i + 1) * TC] for i in range(2)]
    ctqf32 = None
    wbuf = [sb(f"wbuf{i}", [128, 8 * 128], BF16) for i in range(2)]
    onest = sb("onest", [128, 128], F32)
    vp = sb("vp", [128, 80], F32)
    vbias = sb("vbias", [128, D], F32)
    cv = sb("cv", [128, 4], F32)
    strow = sb("strow", [1, 2 * T], F32)
    ps = [es.enter_context(nc.psum_tensor(f"ps{i}", [128, TC], F32)) for i in range(8)]

    st_mu = strow[0:1, 0:T]
    st_tmp = strow[0:1, T:2 * T]
    ctqf32 = ctq[:, :].bitcast(F32)
    sqst = [ctqf32[:, i * T:(i + 1) * T] for i in range(2)]
    sqs = lambda par, j: ctqf32[:, par * T + j * TC: par * T + (j + 1) * TC]
    qp = ctq[:, 0:T]
    kp = ctq[:, T:2 * T]
    atT = lambda par: ctq[:, 2 * T + par * 1024: 2 * T + (par + 1) * 1024]
    ao = h

    VI = {k: i for i, k in enumerate(
        ["dqb", "uqb", "qrb", "qrsb", "krb", "krsb", "ukb", "wob", "dkvb"])}
    def vps(name, ft):
        return vp[:, VI[name] * 8 + ft: VI[name] * 8 + ft + 1]

    for s in ["sIN", "sW", "sPE", "sSC", "sVE", "sGP", "sOUT", "sVV"]:
        p.new_sem(s)

    p.E("sync", lambda e: e.dma_start(out=onest[:], in_=cones), inc="sIN", amt=16)
    p.E("sync", lambda e: e.dma_start(out=vp[:], in_=vpack), inc="sIN", amt=16)
    p.E("sync", lambda e: e.dma_start(out=cv[:], in_=cvals), inc="sIN", amt=16)
    p.E("sync", lambda e: e.dma_start(out=vbias[:], in_=vbias_d), inc="sIN", amt=16)
    n_in0 = p.n("sIN")

    xld = []
    for ft in range(NF):
        par = ft % 2
        tb32 = t32a if par == 0 else t32b
        w = [("sPE", p.aux.get(("st", ft - 2), ("sPE", 0))[1])] if ft >= 2 else []
        p.E("sync", lambda e, ft=ft, tb32=tb32: e.dma_start(out=tb32[:], in_=x2[ft * 128:(ft + 1) * 128, :]),
            waits=w, inc="sIN", amt=16)
        xld.append(p.n("sIN"))
        start = ft == 0
        stop = ft == NF - 1
        wsq = [("sIN", xld[ft])]
        if ft >= 2:
            wsq.append(p.aux[("st", ft - 2)])
        p.E("scalar", lambda e, tb32=tb32, par=par: e.square(sqst[par], tb32[:]),
            waits=wsq, inc="sSC")
        sc_j = p.n("sSC")
        p.W("tensor", [("sSC", sc_j)])
        for j in range(NCH):
            p.E("tensor", lambda e, tb32=tb32, j=j, start=start, stop=stop:
                e.matmul(ps[j][0:1, :], onest[:, 0:1], tb32[:, j * TC:(j + 1) * TC],
                         start=start, stop=stop))
            p.E("tensor", lambda e, j=j, par=par, start=start, stop=stop:
                e.matmul(ps[4 + j][0:1, :], onest[:, 0:1], sqs(par, j),
                         start=start, stop=stop),
                inc="sPE" if j == NCH - 1 else None)
        p.aux[("st", ft)] = ("sPE", p.n("sPE"))
    pe_stats = p.n("sPE")
    for j in range(NCH):
        p.E("vector", lambda e, j=j: e.tensor_scalar_mul(st_mu[:, j * TC:(j + 1) * TC],
                                                         ps[j][0:1, :], 1.0 / D),
            waits=[("sPE", pe_stats)] if j == 0 else (), inc="sVE" if j == NCH - 1 else None)
    p.E("scalar", lambda e: e.square(st_tmp, st_mu), waits=[("sVE", p.n("sVE"))], inc="sSC")
    sc_m2 = p.n("sSC")
    for j in range(NCH):
        p.E("vector", lambda e, j=j: e.scalar_tensor_tensor(
            out=st_tmp[:, j * TC:(j + 1) * TC], in0=ps[4 + j][0:1, :], scalar=1.0 / D,
            in1=st_tmp[:, j * TC:(j + 1) * TC], op0=ALU.mult, op1=ALU.subtract),
            waits=[("sSC", sc_m2)] if j == 0 else (), inc="sVE" if j == NCH - 1 else None)
    p.E("scalar", lambda e: e.activation(out=st_tmp, in_=st_tmp, func=AF.Ln, bias=cv[0:1, 0:1]),
        waits=[("sVE", p.n("sVE"))], inc="sSC")
    p.E("scalar", lambda e: e.activation(out=st_tmp, in_=st_tmp, func=AF.Exp, scale=-0.5), inc="sSC")
    sc_rstd = p.n("sSC")
    p.W("tensor", [("sSC", sc_rstd)])
    for j in range(NCH):
        p.E("tensor", lambda e, j=j: e.matmul(ps[j][:], onest[0:1, :], st_mu[:, j * TC:(j + 1) * TC],
                                              start=True, stop=True))
        p.E("tensor", lambda e, j=j: e.matmul(ps[4 + j][:], onest[0:1, :], st_tmp[:, j * TC:(j + 1) * TC],
                                              start=True, stop=True), inc="sPE" if j == NCH - 1 else None)
    pe_bc = p.n("sPE")
    for ft in range(NF):
        par = ft % 2
        tb32 = t32a if par == 0 else t32b
        w = [("sVE", p.aux.get(("hh", ft - 2), ("sVE", 0))[1])] if ft >= 2 else [("sPE", pe_bc)]
        p.E("sync", lambda e, ft=ft, tb32=tb32: e.dma_start(out=tb32[:], in_=x2[ft * 128:(ft + 1) * 128, :]),
            waits=w, inc="sIN", amt=16)
        xld2 = p.n("sIN")
        for j in range(NCH):
            p.E("vector", lambda e, ft=ft, j=j, tb32=tb32: e.tensor_tensor(
                h[ft][:, j * TC:(j + 1) * TC], tb32[:, j * TC:(j + 1) * TC], ps[j][:], ALU.subtract),
                waits=[("sIN", xld2)] if j == 0 else ())
            p.E("vector", lambda e, ft=ft, j=j: e.tensor_tensor(
                h[ft][:, j * TC:(j + 1) * TC], h[ft][:, j * TC:(j + 1) * TC], ps[4 + j][:], ALU.mult),
                inc="sVE")
        p.aux[("hh", ft)] = ("sVE", p.n("sVE"))
    ve_h = p.n("sVE")
    for slot in range(8):
        p.slot_rel[slot] = ("sVE", ve_h)

    def proj(wsrc, rhs_tiles, kin, nout, evac, tag):
        for o in range(nout):
            bid = o % 2
            w = [p.wbuf_rel[bid]] if bid in p.wbuf_rel else []
            p.E("gpsimd", lambda e, o=o, bid=bid, kin=kin, wsrc=wsrc:
                e.dma_start(out=wbuf[bid][:, 0:kin * 128],
                            in_=wsrc[o * 128:(o + 1) * 128, 0:kin * 128]),
                waits=w, inc="sW", amt=16)
            wrdy = p.n("sW")
            for j in range(NCH):
                slot = (o * NCH + j) % 6
                w2 = [("sW", wrdy)]
                if slot in p.slot_rel:
                    w2.append(p.slot_rel[slot])
                p.W("tensor", w2)
                for kt in range(kin):
                    p.E("tensor", lambda e, kt=kt, j=j, bid=bid, slot=slot, kin=kin:
                        e.matmul(ps[slot][:], wbuf[bid][:, kt * 128:(kt + 1) * 128],
                                 rhs_tiles(kt)[:, j * TC:(j + 1) * TC],
                                 start=(kt == 0), stop=(kt == kin - 1)),
                        inc="sPE" if kt == kin - 1 else None)
                mark = p.n("sPE")
                if j == NCH - 1:
                    p.wbuf_rel[bid] = ("sPE", mark)
                p.slot_rel[slot] = evac(o, j, ps[slot][:], mark)

    def mk_evac_simple(dst, bias_name):
        def evac(o, j, psap, mark):
            p.E("vector", lambda e, o=o, j=j, psap=psap: e.tensor_scalar(
                out=dst(o)[:, j * TC:(j + 1) * TC], in0=psap,
                scalar1=vps(bias_name, o), scalar2=None, op0=ALU.add),
                waits=[("sPE", mark)], inc="sVE")
            return ("sVE", p.n("sVE"))
        return evac

    ctq_t = lambda o: ctq[:, o * T:(o + 1) * T]
    p.W("tensor", [("sVE", ve_h)])
    proj(wdq, lambda kt: h[kt], NF, NQ, mk_evac_simple(ctq_t, "dqb"), "dq")

    qry = lambda o: query[:, o * T:(o + 1) * T]
    kky = lambda o: key[:, o * T:(o + 1) * T]
    p.W("tensor", [("sVE", p.n("sVE"))])
    proj(wuq, ctq_t, NQ, NF, mk_evac_simple(qry, "uqb"), "uq")

    def rope_pass(wsrc, wsrc_s, rhs_tiles, kin, dst, bias_n, bias_sn, acc):
        for o in range(NF):
            tw = [p.aux[("tab", o - 1)]] if ("tab", o - 1) in p.aux else []
            p.E("gpsimd", lambda e, o=o: e.dma_start(out=tabc[:], in_=ctab[o * 128:(o + 1) * 128, :]),
                waits=tw, inc="sW", amt=16)
            p.E("gpsimd", lambda e, o=o: e.dma_start(out=tabs[:], in_=stab[o * 128:(o + 1) * 128, :]),
                inc="sW", amt=16)
            trdy2 = p.n("sW")
            w = [p.wbuf_rel[b] for b in (0, 1) if b in p.wbuf_rel]
            p.E("gpsimd", lambda e, o=o: e.dma_start(
                out=wbuf[0][:, 0:kin * 128], in_=wsrc[o * 128:(o + 1) * 128, 0:kin * 128]),
                waits=w, inc="sW", amt=16)
            p.E("gpsimd", lambda e, o=o: e.dma_start(
                out=wbuf[1][:, 0:kin * 128], in_=wsrc_s[o * 128:(o + 1) * 128, 0:kin * 128]),
                inc="sW", amt=16)
            wrdy = p.n("sW")
            for j in range(NCH):
                slot = (o * NCH + j) % 3
                sa, sb_ = 2 * slot, 2 * slot + 1
                w2 = [("sW", wrdy), ("sW", trdy2)]
                for sl in (sa, sb_):
                    if sl in p.slot_rel:
                        w2.append(p.slot_rel[sl])
                p.W("tensor", w2)
                for kt in range(kin):
                    p.E("tensor", lambda e, kt=kt, j=j, sa=sa:
                        e.matmul(ps[sa][:], wbuf[0][:, kt * 128:(kt + 1) * 128],
                                 rhs_tiles(kt)[:, j * TC:(j + 1) * TC],
                                 start=(kt == 0), stop=(kt == kin - 1)))
                for kt in range(kin):
                    p.E("tensor", lambda e, kt=kt, j=j, sb_=sb_, kin=kin:
                        e.matmul(ps[sb_][:], wbuf[1][:, kt * 128:(kt + 1) * 128],
                                 rhs_tiles(kt)[:, j * TC:(j + 1) * TC],
                                 start=(kt == 0), stop=(kt == kin - 1)),
                        inc="sPE" if kt == kin - 1 else None)
                mark = p.n("sPE")
                jj = slice(j * TC, (j + 1) * TC)
                p.E("vector", lambda e, o=o, j=j, sa=sa, jj=jj: e.tensor_scalar(
                    out=xst[0], in0=ps[sa][:], scalar1=vps(bias_n, o), scalar2=None, op0=ALU.add),
                    waits=[("sPE", mark)])
                p.E("vector", lambda e, o=o, j=j, jj=jj: e.tensor_tensor(
                    xst[0], xst[0], tabc[:, jj], ALU.mult))
                p.E("vector", lambda e, o=o, j=j, sb_=sb_, jj=jj: e.tensor_scalar(
                    out=xst[1], in0=ps[sb_][:], scalar1=vps(bias_sn, o), scalar2=None, op0=ALU.add))
                p.E("vector", lambda e, o=o, j=j, jj=jj: e.tensor_tensor(
                    xst[1], xst[1], tabs[:, jj], ALU.mult))
                p.E("vector", lambda e: e.tensor_tensor(xst[0], xst[0], xst[1], ALU.add))
                if acc:
                    p.E("vector", lambda e, o=o, jj=jj: e.tensor_tensor(
                        dst(o)[:, jj], dst(o)[:, jj], xst[0], ALU.add), inc="sVE")
                else:
                    p.E("vector", lambda e, o=o, jj=jj: e.tensor_copy(dst(o)[:, jj], xst[0]),
                        inc="sVE")
                rel = ("sVE", p.n("sVE"))
                p.slot_rel[sa] = rel
                p.slot_rel[sb_] = rel
            p.aux[("tab", o)] = ("sVE", p.n("sVE"))
            p.wbuf_rel[0] = ("sPE", p.n("sPE"))
            p.wbuf_rel[1] = ("sPE", p.n("sPE"))
        for o in range(NF):
            p.aux.pop(("tab", o), None)

    rope_pass(wqr, wqrs, ctq_t, NQ, qry, "qrb", "qrsb", acc=True)

    p.W("tensor", [("sVE", p.n("sVE"))])
    proj(wdkv, lambda kt: h[kt], NF, NQ, mk_evac_simple(ctq_t, "dkvb"), "dkv")
    p.W("tensor", [("sVE", p.n("sVE"))])
    proj(wuk, ctq_t, NQ, NF, mk_evac_simple(kky, "ukb"), "uk")
    rope_pass(wkr, wkrs, lambda kt: h[kt], NF, kky, "krb", "krsb", acc=True)

    if DEBUG_L1A:
        ve_now = p.n("sVE")
        for ft in range(NF):
            p.E("gpsimd", lambda e, ft=ft: e.dma_start(out=dbg_h[ft * 128:(ft + 1) * 128, :], in_=h[ft][:]),
                waits=[("sVE", ve_now)] if ft == 0 else (), inc="sGP", amt=16)
        for o in range(NQ):
            p.E("gpsimd", lambda e, o=o: e.dma_start(out=dbg_ctq[o * 128:(o + 1) * 128, :], in_=ctq_t(o)),
                inc="sGP", amt=16)
        for ft in range(NF):
            p.E("gpsimd", lambda e, ft=ft: e.dma_start(out=dbg_q[ft * 128:(ft + 1) * 128, :], in_=qry(ft)),
                inc="sGP", amt=16)
            p.E("gpsimd", lambda e, ft=ft: e.dma_start(out=dbg_k[ft * 128:(ft + 1) * 128, :], in_=kky(ft)),
                inc="sGP", amt=16)
        p.W("gpsimd", [("sGP", p.n("sGP"))])

    p.E("vector", lambda e: e.memset(vtm[:], 1.0), inc="sVE")
    p.W("tensor", [("sVE", p.n("sVE"))])
    for tt in range(16):
        for oc in range(2):
            slot = (tt * 2 + oc) % 6
            w2 = []
            if slot in p.slot_rel:
                w2.append(p.slot_rel[slot])
            wvbuf = tabc if oc == 0 else tabs
            if ("wuv", oc) not in p.aux:
                p.E("gpsimd", lambda e, oc=oc, wvbuf=wvbuf: e.dma_start(
                    out=wvbuf[:, 0:4 * TC].rearrange("p (kt o) -> p kt o", o=TC),
                    in_=wuv[0:DQ, oc * TC:(oc + 1) * TC].rearrange("(kt p) o -> p kt o", p=128)),
                    waits=[p.wbuf_rel[b] for b in (0, 1) if b in p.wbuf_rel] if oc == 0 else (),
                    inc="sW", amt=16)
                p.aux[("wuv", oc)] = p.n("sW")
            w2.append(("sW", p.aux[("wuv", oc)]))
            p.W("tensor", w2)
            for kt in range(NQ):
                p.E("tensor", lambda e, tt=tt, oc=oc, kt=kt, slot=slot, wvbuf=wvbuf:
                    e.matmul(ps[slot][:], ctq_t(kt)[:, tt * 128:(tt + 1) * 128],
                             wvbuf[:, kt * TC:(kt + 1) * TC],
                             start=(kt == 0), stop=(kt == NQ - 1)),
                    inc="sPE" if kt == NQ - 1 else None)
            mark = p.n("sPE")
            p.E("vector", lambda e, tt=tt, oc=oc, slot=slot: e.tensor_tensor(
                xst[0], ps[slot][:], vbias[:, oc * TC:(oc + 1) * TC], ALU.add),
                waits=[("sPE", mark)])
            for hh in range(8):
                hgl = oc * 8 + hh
                p.E("vector", lambda e, tt=tt, hh=hh, hgl=hgl: e.tensor_copy(
                    vtm[:, tt * (NH * 65) + hgl * 65: tt * (NH * 65) + hgl * 65 + 64],
                    xst[0][:, hh * 64:(hh + 1) * 64]),
                    inc="sVE" if hh == 7 else None)
            p.slot_rel[slot] = ("sVE", p.n("sVE"))
    ve_vtm = p.n("sVE")

    sq32 = t32a
    qn2 = st_mu
    kn2 = st_tmp
    kmaxr = cv[0:1, 2:3]
    qmaxr = cv[0:1, 3:4]
    rsum = t32b[0:1, 0:T]
    stage = ctq[:, 3 * T:4 * T]
    biasb = vp[:, 72:73]
    p.W("tensor", [("sVE", ve_vtm)])
    for hh in range(NH):
        fh = hh // 2
        base = 64 * (hh % 2)
        wpre = [("sVE", p.n("sVE"))]
        if ("hd", hh - 1) in p.aux:
            wpre.append(p.aux[("hd", hh - 1)])
        p.E("scalar", lambda e, fh=fh, base=base: e.square(
            sq32[base:base + 64, :], query[base:base + 64, fh * T:(fh + 1) * T]),
            waits=wpre, inc="sSC")
        sc_q2 = p.n("sSC")
        for j in range(NCH):
            sl = 6 + j % 2
            w2 = [("sSC", sc_q2)]
            if ("n2", sl) in p.aux:
                w2.append(p.aux[("n2", sl)])
            p.W("tensor", w2)
            p.E("tensor", lambda e, j=j, sl=sl, base=base: e.matmul(
                ps[sl][0:1, :], onest[base:base + 64, 0:1],
                sq32[base:base + 64, j * TC:(j + 1) * TC],
                start=True, stop=True), inc="sPE")
            pe_m = p.n("sPE")
            p.EV("vector", lambda e, j=j, sl=sl: e.tensor_copy(qn2[0:1, j * TC:(j + 1) * TC],
                                                               ps[sl][0:1, 0:TC]),
                 waits=[("sPE", pe_m)])
            p.aux[("n2", sl)] = ("sVV", p.n("sVV"))
        p.E("scalar", lambda e, fh=fh, base=base: e.square(
            sq32[base:base + 64, :], key[base:base + 64, fh * T:(fh + 1) * T]),
            waits=[("sVV", p.n("sVV"))], inc="sSC")
        sc_k2 = p.n("sSC")
        for j in range(NCH):
            sl = 6 + j % 2
            p.W("tensor", [("sSC", sc_k2), p.aux[("n2", sl)]])
            p.E("tensor", lambda e, j=j, sl=sl, base=base: e.matmul(
                ps[sl][0:1, :], onest[base:base + 64, 0:1],
                sq32[base:base + 64, j * TC:(j + 1) * TC],
                start=True, stop=True), inc="sPE")
            pe_m = p.n("sPE")
            p.EV("vector", lambda e, j=j, sl=sl: e.tensor_copy(kn2[0:1, j * TC:(j + 1) * TC],
                                                               ps[sl][0:1, 0:TC]),
                 waits=[("sPE", pe_m)])
            p.aux[("n2", sl)] = ("sVV", p.n("sVV"))
        p.EV("vector", lambda e: e.reduce_max(kmaxr, kn2, axis=AX.X),
             waits=[("sVV", p.n("sVV"))])
        p.EV("vector", lambda e: e.reduce_max(qmaxr, qn2, axis=AX.X))
        p.EV("vector", lambda e: e.tensor_tensor(qmaxr, qmaxr, kmaxr, ALU.mult))
        vv_m = p.n("sVV")
        p.E("scalar", lambda e: e.activation(out=qmaxr, in_=qmaxr, func=AF.Sqrt),
            waits=[("sVV", vv_m)], inc="sSC")
        sc_m = p.n("sSC")
        sl_b = 6
        p.W("tensor", [("sSC", sc_m), p.aux[("n2", 6)]])
        p.E("tensor", lambda e: e.matmul(ps[sl_b][:, 0:1], onest[0:1, :], qmaxr,
                                         start=True, stop=True), inc="sPE")
        pe_bb = p.n("sPE")
        p.EV("vector", lambda e: e.tensor_scalar_mul(biasb, ps[sl_b][:, 0:1], -1.0 / SCL),
             waits=[("sPE", pe_bb)])
        p.aux[("n2", 6)] = ("sVV", p.n("sVV"))
        p.E("vector", lambda e: None, waits=[("sVV", p.n("sVV"))], inc="sVE")
        ve_qk = p.n("sVE")
        p.W("tensor", [("sVE", ve_qk)])
        p.W("scalar", [("sVE", ve_qk)])
        for kt in range(16):
            for half in range(2):
                it = kt * 2 + half
                par = it % 2
                ssl = 2 * par
                w2 = []
                if ("sc", ssl) in p.aux:
                    w2.append(p.aux[("sc", ssl)])
                if w2:
                    p.W("tensor", w2)
                for n in range(2):
                    p.E("tensor", lambda e, kt=kt, half=half, n=n, ssl=ssl, fh=fh, base=base: e.matmul(
                        ps[ssl + n][:], key[base:base + 64, fh * T + kt * 128: fh * T + (kt + 1) * 128],
                        query[base:base + 64, fh * T + half * 1024 + n * TC: fh * T + half * 1024 + (n + 1) * TC],
                        start=True, stop=True),
                        inc="sPE" if n == 1 else None)
                pe_sc = p.n("sPE")
                wexp = [("sPE", pe_sc)]
                if ("av", par) in p.aux:
                    wexp.append(p.aux[("av", par)])
                for n in range(2):
                    p.E("scalar", lambda e, n=n, par=par, ssl=ssl: e.activation(
                        out=atT(par)[:, n * TC:(n + 1) * TC], in_=ps[ssl + n][:],
                        func=AF.Exp, scale=1.0 / SCL, bias=biasb),
                        waits=wexp if n == 0 else (), inc="sSC" if n == 1 else None)
                sc_exp = p.n("sSC")
                p.aux[("sc", ssl)] = ("sSC", sc_exp)
                p.W("tensor", [("sSC", sc_exp)])
                for n in range(2):
                    p.E("tensor", lambda e, kt=kt, half=half, n=n, par=par, hh=hh: e.matmul(
                        ps[4 + half * 2 + n][0:65, :],
                        vtm[:, kt * (NH * 65) + hh * 65: kt * (NH * 65) + (hh + 1) * 65],
                        atT(par)[:, n * TC:(n + 1) * TC],
                        start=(kt == 0), stop=(kt == 15)),
                        inc="sPE" if n == 1 else None)
                p.aux[("av", par)] = ("sPE", p.n("sPE"))
        pe_av = p.n("sPE")
        for j in range(NCH):
            p.EV("vector", lambda e, j=j: e.reciprocal(rsum[0:1, j * TC:(j + 1) * TC],
                                                       ps[4 + j][64:65, 0:TC]),
                 waits=[("sPE", pe_av)] if j == 0 else ())
        p.E("vector", lambda e: None, waits=[("sVV", p.n("sVV"))], inc="sVE")
        ve_rs = p.n("sVE")
        p.W("tensor", [("sVE", ve_rs)])
        for j in range(NCH):
            p.E("tensor", lambda e, j=j: e.matmul(ps[j][0:64, :], onest[0:1, 0:64],
                                                  rsum[0:1, j * TC:(j + 1) * TC],
                                                  start=True, stop=True),
                inc="sPE" if j == NCH - 1 else None)
        pe_b = p.n("sPE")
        dst = (lambda j, fh=fh: ao[fh][0:64, j * TC:(j + 1) * TC]) if base == 0 else (lambda j: stage[0:64, j * TC:(j + 1) * TC])
        for j in range(NCH):
            p.E("vector", lambda e, j=j: e.tensor_copy(sq32[0:64, j * TC:(j + 1) * TC],
                                                       ps[4 + j][0:64, 0:TC]),
                waits=[("sPE", pe_b)] if j == 0 else ())
            p.E("vector", lambda e, j=j, dst=dst: e.tensor_tensor(
                dst(j), sq32[0:64, j * TC:(j + 1) * TC], ps[j][0:64, 0:TC], ALU.mult),
                inc="sVE" if j == NCH - 1 else None)
        if base == 64:
            p.E("gpsimd", lambda e, fh=fh: e.dma_start(out=ao[fh][64:128, 0:T], in_=stage[0:64, 0:T]),
                waits=[("sVE", p.n("sVE"))], inc="sGP", amt=16)
            p.aux[("hd", hh)] = ("sGP", p.n("sGP"))
        else:
            p.aux[("hd", hh)] = ("sVE", p.n("sVE"))
        for sl in range(8):
            p.slot_rel[sl] = p.aux[("hd", hh)]
        p.aux.pop(("av", 0), None)
        p.aux.pop(("av", 1), None)
        p.aux.pop(("sc", 0), None)
        p.aux.pop(("sc", 2), None)
    p.W("tensor", [p.aux[("hd", NH - 1)]])
    p.E("vector", lambda e: None, waits=[p.aux[("hd", NH - 1)]], inc="sVE")
    if DEBUG_L1A:
        p.E("gpsimd", lambda e: e.dma_start(out=dbg_v, in_=vtm[:]),
            waits=[("sVE", p.n("sVE"))], inc="sGP", amt=16)
        for ft in range(NF):
            p.E("gpsimd", lambda e, ft=ft: e.dma_start(out=dbg_ao[ft * 128:(ft + 1) * 128, :], in_=ao[ft][:]),
                inc="sGP", amt=16)
        p.W("gpsimd", [("sGP", p.n("sGP"))])
    ve_ao = p.n("sVE")

    ao_t = lambda kt: ao[kt][:, 0:T]
    p.W("tensor", [("sVE", ve_ao)])
    for o in range(NF):
        bid = o % 2
        w = [p.wbuf_rel[bid]] if bid in p.wbuf_rel else []
        p.E("gpsimd", lambda e, o=o, bid=bid: e.dma_start(
            out=wbuf[bid][:, 0:NF * 128], in_=wo[o * 128:(o + 1) * 128, 0:NF * 128]),
            waits=w, inc="sW", amt=16)
        wrdy = p.n("sW")
        for j in range(NCH):
            par = (o * NCH + j) % 2
            slot = (o * NCH + j) % 6
            wx = [p.aux[("xi", par)]] if ("xi", par) in p.aux else [("sVE", ve_ao)]
            p.E("sync", lambda e, o=o, j=j, par=par: e.dma_start(
                out=xio[par], in_=x2[o * 128:(o + 1) * 128, j * TC:(j + 1) * TC]),
                waits=wx, inc="sIN", amt=16)
            xrdy = p.n("sIN")
            w2 = [("sW", wrdy)]
            if slot in p.slot_rel:
                w2.append(p.slot_rel[slot])
            p.W("tensor", w2)
            for kt in range(NF):
                p.E("tensor", lambda e, kt=kt, j=j, bid=bid, slot=slot: e.matmul(
                    ps[slot][:], wbuf[bid][:, kt * 128:(kt + 1) * 128],
                    ao_t(kt)[:, j * TC:(j + 1) * TC],
                    start=(kt == 0), stop=(kt == NF - 1)),
                    inc="sPE" if kt == NF - 1 else None)
            mark = p.n("sPE")
            if j == NCH - 1:
                p.wbuf_rel[bid] = ("sPE", mark)
            wv = [("sPE", mark), ("sIN", xrdy)]
            if ("xo", par) in p.aux:
                wv.append(p.aux[("xo", par)])
            p.E("vector", lambda e, o=o, par=par, slot=slot: e.tensor_scalar(
                out=xst[par], in0=ps[slot][:], scalar1=vps("wob", o), scalar2=None,
                op0=ALU.add), waits=wv)
            p.E("vector", lambda e, par=par: e.tensor_tensor(
                xst[par], xst[par], xio[par], ALU.add), inc="sVE")
            ve_o = p.n("sVE")
            p.slot_rel[slot] = ("sVE", ve_o)
            p.aux[("xi", par)] = ("sVE", ve_o)
            p.E("gpsimd", lambda e, o=o, j=j, par=par: e.dma_start(
                out=x3[o * 128:(o + 1) * 128, j * TC:(j + 1) * TC], in_=xst[par]),
                waits=[("sVE", ve_o)], inc="sOUT", amt=16)
            p.aux[("xo", par)] = ("sOUT", p.n("sOUT"))
    p.W("gpsimd", [("sOUT", p.n("sOUT"))])
    p.W("sync", [("sOUT", p.n("sOUT"))])

    p.finalize()
    return nc, es


def _rope_perm():
    P = np.zeros(1024, np.int64)
    idx = 0
    for hh in range(16):
        base = 64 * hh
        P[base:base + 32] = base + 2 * np.arange(32)
        P[base + 32:base + 64] = base + 2 * np.arange(32) + 1
    return P


def _rope_tables():
    ROPE_BASE = 10000.0
    dim = 1024
    pos = np.arange(T, dtype=np.float64)[:, None]
    ids = np.arange(dim // 2, dtype=np.float64)
    theta = ROPE_BASE ** (-2.0 * ids) / (dim // 2)
    r = pos * theta[None, :]
    cosv = np.cos(r).astype(np.float32)
    sinv = np.sin(r).astype(np.float32)
    C = np.zeros((1024, T), np.float32)
    S = np.zeros((1024, T), np.float32)
    for hh in range(16):
        base = 64 * hh
        pr = 32 * hh + np.arange(32)
        C[base:base + 32] = cosv[:, pr].T
        C[base + 32:base + 64] = cosv[:, pr].T
        S[base:base + 32] = -sinv[:, pr].T
        S[base + 32:base + 64] = sinv[:, pr].T
    return C, S


def l1_attn_inputs(core, inp, x2_fm):
    NQ = 4
    P = _rope_perm()
    C, S = _rope_tables()
    SW = np.zeros(1024, np.int64)
    for hh in range(16):
        b = 64 * hh
        SW[b:b + 32] = np.arange(b + 32, b + 64)
        SW[b + 32:b + 64] = np.arange(b, b + 32)
    wqr_p = f32(inp["L1_attn_Wq_r_w"])[:, P]
    wqr_s = wqr_p[:, SW]
    wkr_p = f32(inp["L1_attn_Wk_r_w"])[:, P]
    wkr_s = wkr_p[:, SW]
    bqr = f32(inp["L1_attn_Wq_r_b"])[P]
    bqr_s = bqr[SW]
    bkr = f32(inp["L1_attn_Wk_r_b"])[P]
    bkr_s = bkr[SW]

    def pk(v, n=8):
        v = np.asarray(v, np.float32)
        cols = v.reshape(-1, 128).T
        out = np.zeros((128, 8), np.float32)
        out[:, :cols.shape[1]] = cols
        return out

    vpk = np.concatenate([
        pk(inp["L1_attn_Wd_q_b"]), pk(f32(inp["L1_attn_Wu_q_b"])[P]), pk(bqr), pk(bqr_s),
        pk(bkr), pk(bkr_s), pk(f32(inp["L1_attn_Wu_k_b"])[P]), pk(inp["L1_attn_Wo_b"]),
        pk(inp["L1_attn_Wd_kv_b"]), np.zeros((128, 8), np.float32),
    ], axis=1)
    return {
        "x2": x2_fm[core],
        "wdq": wprep(inp["L1_attn_Wd_q_w"], NF, NQ),
        "wuq": wprep(f32(inp["L1_attn_Wu_q_w"])[:, P], NQ, NF),
        "wqr": wprep(wqr_p, NQ, NF),
        "wqrs": wprep(wqr_s, NQ, NF),
        "wkr": wprep(wkr_p, NF, NF),
        "wkrs": wprep(wkr_s, NF, NF),
        "wdkv": wprep(inp["L1_attn_Wd_kv_w"], NF, NQ),
        "wuk": wprep(f32(inp["L1_attn_Wu_k_w"])[:, P], NQ, NF),
        "wuv": f32(inp["L1_attn_Wu_v_w"]),
        "wo": wprep(inp["L1_attn_Wo_w"], NF, NF),
        "ctab": C, "stab": S,
        "vpack": vpk[:, :80],
        "vbias_d": np.tile(f32(inp["L1_attn_Wu_v_b"]), (128, 1)),
        "cones": np.ones((128, 128), np.float32),
        "cvals": np.tile(np.array([1e-5, 1e-3, 0.0, 0.0], np.float32), (128, 1)),
    }


def run_l1_attn(inp, x2_fm):
    nc, es = build_l1_attn()
    in_maps = [l1_attn_inputs(c, inp, x2_fm) for c in range(N_CORES)]
    res = run_bass_kernel_spmd(nc, in_maps, core_ids=list(range(N_CORES)))
    es.close()
    if DEBUG_L1A:
        return res
    return [res.results[c]["x3"] for c in range(N_CORES)]





QHEAD0 = [0, 1, 2, 3, 8, 9, 10, 11]
QHEAD1 = [4, 5, 6, 7, 12, 13, 14, 15]
DEBUG_L1F = 0


def build_l1_ffn(core):
    nc = bass.Bass("TRN2", target_bir_lowering=False)
    es = ExitStack()
    p = Prog(nc, es)
    E = 16
    NI1 = 16
    CAP = 2560.0
    SCLA = 0.125

    x3 = nc.dram_tensor("x3", [D, T], F32, kind="ExternalInput").ap()
    wq = nc.dram_tensor("wq", [NF * 128, NF * 128], F32, kind="ExternalInput").ap()
    wkv = nc.dram_tensor("wkv", [4 * 128, NF * 128], F32, kind="ExternalInput").ap()
    wpj = nc.dram_tensor("wpj", [NF * 128, NF * 128], F32, kind="ExternalInput").ap()
    w1 = nc.dram_tensor("w1", [E, NI1 * 128, NF * 128], F32, kind="ExternalInput").ap()
    w2 = nc.dram_tensor("w2", [E, NF * 128, NI1 * 128], F32, kind="ExternalInput").ap()
    rwp = nc.dram_tensor("rwp", [128, NF * 16], F32, kind="ExternalInput").ap()
    vpack = nc.dram_tensor("vpack", [128, 40], F32, kind="ExternalInput").ap()
    cones = nc.dram_tensor("cones", [128, 128], F32, kind="ExternalInput").ap()
    cident = nc.dram_tensor("cident", [128, 128], F32, kind="ExternalInput").ap()
    ciota = nc.dram_tensor("ciota", [128, 16], F32, kind="ExternalInput").ap()
    cbig = nc.dram_tensor("cbig", [128, 16], F32, kind="ExternalInput").ap()
    coh16 = nc.dram_tensor("coh16", [16, 16 * 128], F32, kind="ExternalInput").ap()
    cblkq = nc.dram_tensor("cblkq", [128, NF * 16], F32, kind="ExternalInput").ap()
    cblkk = nc.dram_tensor("cblkk", [128, 2 * 16], F32, kind="ExternalInput").ap()
    cblkqT = nc.dram_tensor("cblkqT", [16, NF * 128], F32, kind="ExternalInput").ap()
    cmask = nc.dram_tensor("cmask", [16, 8], F32, kind="ExternalInput").ap()
    cpfx = nc.dram_tensor("cpfx", [128, 16], F32, kind="ExternalInput").ap()
    out_d = nc.dram_tensor("out", [D, T], F32, kind="ExternalOutput").ap()
    if DEBUG_L1F:
        dbg_q = nc.dram_tensor("dbg_q", [D, T], F32, kind="ExternalOutput").ap()
        dbg_kv = nc.dram_tensor("dbg_kv", [512, T], F32, kind="ExternalOutput").ap()
        dbg_E = nc.dram_tensor("dbg_E", [16, 8 * T], F32, kind="ExternalOutput").ap()
        dbg_acc = nc.dram_tensor("dbg_acc", [D, T], F32, kind="ExternalOutput").ap()
        dbg_x4 = nc.dram_tensor("dbg_x4", [D, T], F32, kind="ExternalOutput").ap()
        dbg_cw = nc.dram_tensor("dbg_cw", [128, 256], F32, kind="ExternalOutput").ap()
        dbg_off = nc.dram_tensor("dbg_off", [128, 16], F32, kind="ExternalOutput").ap()
        dbg_ln2 = nc.dram_tensor("dbg_ln2", [D, T], F32, kind="ExternalOutput").ap()
        dbg_l1 = nc.dram_tensor("dbg_l1", [D, T], F32, kind="ExternalOutput").ap()
        dbg_h0 = nc.dram_tensor("dbg_h0", [D, T], F32, kind="ExternalOutput").ap()
    kvin_t = nc.dram_tensor("kvin", [512, T], BF16)
    kvout_t = nc.dram_tensor("kvout", [8 * 512, T], BF16)
    ctin_t = nc.dram_tensor("ctin", [16, 1], F32)
    ctout_t = nc.dram_tensor("ctout", [8 * 16, 1], F32)
    kvin = kvin_t.ap()
    kvout = kvout_t.ap()
    ctin = ctin_t.ap()
    ctout = ctout_t.ap()

    sb = lambda name, shape, dt: es.enter_context(nc.sbuf_tensor(name, shape, dt))
    ARB = sb("ARB", [128, 16 * T], BF16)
    h = [sb(f"h{i}", [128, T], BF16) for i in range(NF)]
    lac = [sb(f"lac{i}", [128, T], BF16) for i in range(NF)]
    kva = sb("kva", [128, 8 * T], BF16)
    T32 = sb("T32", [128, 2 * T], F32)
    wbuf = [sb(f"wbuf{i}", [128, 8 * 128], BF16) for i in range(2)]
    onest = sb("onest", [128, 128], F32)
    identt = sb("identt", [128, 128], F32)
    vp = sb("vp", [128, 40], F32)
    iot = sb("iot", [128, 16], F32)
    big = sb("big", [128, 16], F32)
    oh16 = sb("oh16", [16, 16 * 128], BF16)
    blkq = sb("blkq", [128, NF * 16], BF16)
    blkqT = sb("blkqT", [16, NF * 128], BF16)
    maskt = sb("maskt", [16, 8], F32)
    pfx = sb("pfx", [128, 16], F32)
    ctsb = sb("ctsb", [128, 1], F32)
    offb = sb("offb", [128, 16], F32)
    prb = sb("prb", [128, 16 * 16], F32)
    oh0 = sb("oh0", [128, 16 * 16], F32)
    oh1 = sb("oh1", [128, 16 * 16], F32)
    gat = sb("gat", [128, 2 * 16], F32)
    cwall = sb("cwall", [128, 16 * 16], F32)
    ohk = sb("ohk", [128, 16], F32)
    ohw = sb("ohw", [128, 16], F32)
    msk = sb("msk", [128, 16], I32)
    mrow = sb("mrow", [128, 4], F32)
    cwb = sb("cwb", [128, T], BF16)
    cwT = sb("cwT", [16, T], BF16)
    rw = sb("rw", [128, NF * 16], F32)
    s1b = sb("s1b", [128, 16], F32)
    s2b = sb("s2b", [128, 16], F32)
    murc = sb("murc", [128, 4], F32)
    lgt = sb("lgt", [128, 16], F32)
    blkk = sb("blkk", [128, 2 * 16], BF16)
    ctrow = sb("ctrow", [16, 1], F32)
    ctrow2 = sb("ctrow2", [16, 1], F32)
    ctrow3 = sb("ctrow3", [1, 16], F32)
    exT = sb("exT", [128, 16], F32)
    ps = [es.enter_context(nc.psum_tensor(f"ps{i}", [128, TC], F32)) for i in range(8)]

    ARBf = ARB[:, :].bitcast(F32)
    q_t = lambda ft: ARB[:, ft * T:(ft + 1) * T]
    x4_t = lambda ft: ARBf[:, ft * T:(ft + 1) * T]
    Erow = lambda s_: ARB[0:16, (NF + s_) * T:(NF + s_ + 1) * T]
    T32f = T32[:, :].bitcast(F32) if False else T32
    st_mu = T32[0:1, 0:T]
    stag = [T32[:, i * T:(i + 1) * T] for i in range(2)]
    kcur = lambda j: kva[:, j * T:(j + 1) * T]
    Mrow = lambda: T32[0:16, 0:T]
    Krow = lambda: T32[0:16, T:2 * T]
    Zrow = T32[0:16, 0:T]
    ZInc = T32[0:16, T:2 * T]
    kvf = kva[:, :].bitcast(F32)
    Zzero = kvf[0:16, 0:T]
    strow2 = kvf[0:1, 2 * T:4 * T]
    T32f = T32
    xio = [T32[:, i * TC:(i + 1) * TC] for i in range(2)]
    xst = [T32[:, T + i * TC: T + (i + 1) * TC] for i in range(2)]
    T32b16 = T32[:, :].bitcast(BF16)
    act = lambda it: T32b16[:, it * TC:(it + 1) * TC]
    bg1k = lambda it, kt: lac[(it * NF + kt) // 16][:, ((it * NF + kt) % 16) * 128:
                                                    (((it * NF + kt) % 16) + 1) * 128]
    bg1s = lambda it: lac[(it * NF) // 16][:, ((it * NF) % 16) * 128: ((it * NF) % 16) * 128 + NF * 128]
    bd2s = lambda o: kva[:, o * NI1 * 128:(o + 1) * NI1 * 128]
    bd2k = lambda o, kt: kva[:, (o * NI1 + kt) * 128:(o * NI1 + kt + 1) * 128]
    blkqk = lambda j2: blkk[:, j2 * 16:(j2 + 1) * 16]

    VI = {k: i for i, k in enumerate(["l1w", "l1b", "l2w", "l2b"])}
    vps = lambda name, ft: vp[:, VI[name] * 8 + ft: VI[name] * 8 + ft + 1]

    for s in ["sIN", "sW", "sPE", "sSC", "sVE", "sGP", "sOUT", "sVV", "sCC"]:
        p.new_sem(s)

    for nm, tile in [(cones, onest), (cident, identt), (ciota, iot), (cbig, big),
                     (cmask, maskt), (cpfx, pfx), (vpack, vp)]:
        p.E("sync", lambda e, nm=nm, tile=tile: e.dma_start(out=tile[:], in_=nm), inc="sIN", amt=16)
    p.E("gpsimd", lambda e: e.dma_start(out=oh16[:], in_=coh16), inc="sIN", amt=16)
    p.E("gpsimd", lambda e: e.dma_start(out=blkq[:], in_=cblkq), inc="sIN", amt=16)
    p.E("gpsimd", lambda e: e.dma_start(out=blkqT[:], in_=cblkqT), inc="sIN", amt=16)
    p.E("gpsimd", lambda e: e.dma_start(out=blkk[:], in_=cblkk), inc="sIN", amt=16)
    n_in0 = p.n("sIN")

    sqst = [ARBf[:, i * T:(i + 1) * T] for i in range(2)]
    xld = []
    for ft in range(NF):
        par = ft % 2
        tb32 = stag[par]
        w = [p.aux[("st", ft - 2)]] if ft >= 2 else []
        p.E("sync", lambda e, ft=ft, tb32=tb32: e.dma_start(out=tb32, in_=x3[ft * 128:(ft + 1) * 128, :]),
            waits=w, inc="sIN", amt=16)
        xld.append(p.n("sIN"))
        start, stop = ft == 0, ft == NF - 1
        wsq = [("sIN", xld[ft])]
        if ft >= 2:
            wsq.append(p.aux[("st", ft - 2)])
        p.E("scalar", lambda e, tb32=tb32, par=par: e.square(sqst[par], tb32),
            waits=wsq, inc="sSC")
        sc_j = p.n("sSC")
        p.W("tensor", [("sSC", sc_j)])
        for j in range(NCH):
            p.E("tensor", lambda e, tb32=tb32, j=j, start=start, stop=stop:
                e.matmul(ps[j][0:1, :], onest[:, 0:1], tb32[:, j * TC:(j + 1) * TC],
                         start=start, stop=stop))
            p.E("tensor", lambda e, j=j, par=par, start=start, stop=stop:
                e.matmul(ps[4 + j][0:1, :], onest[:, 0:1],
                         ARBf[:, par * T + j * TC: par * T + (j + 1) * TC],
                         start=start, stop=stop),
                inc="sPE" if j == NCH - 1 else None)
        p.aux[("st", ft)] = ("sPE", p.n("sPE"))
    pe_stats = p.n("sPE")
    r_mu = ARBf[0:1, 2 * T:3 * T]
    r_tmp = ARBf[0:1, 3 * T:4 * T]
    r_r0 = ARBf[0:1, 4 * T:5 * T]
    r_r01 = ARBf[0:1, 5 * T:6 * T]
    for j in range(NCH):
        p.E("vector", lambda e, j=j: e.tensor_scalar_mul(r_mu[:, j * TC:(j + 1) * TC],
                                                         ps[j][0:1, :], 1.0 / D),
            waits=[("sPE", pe_stats)] if j == 0 else (), inc="sVE" if j == NCH - 1 else None)
    p.E("scalar", lambda e: e.square(r_tmp, r_mu), waits=[("sVE", p.n("sVE"))], inc="sSC")
    sc_m2 = p.n("sSC")
    for j in range(NCH):
        p.E("vector", lambda e, j=j: e.scalar_tensor_tensor(
            out=r_tmp[:, j * TC:(j + 1) * TC], in0=ps[4 + j][0:1, :], scalar=1.0 / D,
            in1=r_tmp[:, j * TC:(j + 1) * TC], op0=ALU.mult, op1=ALU.subtract),
            waits=[("sSC", sc_m2)] if j == 0 else (), inc="sVE" if j == NCH - 1 else None)
    epsap = vp[0:1, 32:33]
    p.E("scalar", lambda e: e.activation(out=r_r01, in_=r_tmp, func=AF.Ln, bias=epsap),
        waits=[("sVE", p.n("sVE"))], inc="sSC")
    p.E("scalar", lambda e: e.activation(out=r_r0, in_=r_r01, func=AF.Exp, scale=-0.5), inc="sSC")
    sc_r0 = p.n("sSC")
    p.E("vector", lambda e: e.tensor_mul(r_tmp, r_tmp, r_r0), waits=[("sSC", sc_r0)])
    p.E("vector", lambda e: e.tensor_mul(r_tmp, r_tmp, r_r0), inc="sVE")
    p.E("scalar", lambda e: e.activation(out=r_r01, in_=r_tmp, func=AF.Ln, bias=epsap),
        waits=[("sVE", p.n("sVE"))])
    p.E("scalar", lambda e: e.activation(out=r_r01, in_=r_r01, func=AF.Exp, scale=-0.5), inc="sSC")
    p.E("vector", lambda e: None, waits=[("sSC", p.n("sSC"))], inc="sVE")
    ve_r = p.n("sVE")
    p.W("tensor", [("sVE", ve_r)])
    for j in range(NCH):
        p.E("tensor", lambda e, j=j: e.matmul(ps[j][:], onest[0:1, :], r_mu[:, j * TC:(j + 1) * TC],
                                              start=True, stop=True))
        p.E("tensor", lambda e, j=j: e.matmul(ps[4 + j][:], onest[0:1, :], r_r0[:, j * TC:(j + 1) * TC],
                                              start=True, stop=True), inc="sPE" if j == NCH - 1 else None)
    pe_bc = p.n("sPE")
    for ft in range(NF):
        par = ft % 2
        tb32 = stag[par]
        w = [p.aux[("hh", ft - 2)]] if ft >= 2 else [("sPE", pe_bc)]
        p.E("sync", lambda e, ft=ft, tb32=tb32: e.dma_start(out=tb32, in_=x3[ft * 128:(ft + 1) * 128, :]),
            waits=w, inc="sIN", amt=16)
        xl2 = p.n("sIN")
        for j in range(NCH):
            p.E("vector", lambda e, ft=ft, j=j, tb32=tb32: e.tensor_tensor(
                h[ft][:, j * TC:(j + 1) * TC], tb32[:, j * TC:(j + 1) * TC], ps[j][:], ALU.subtract),
                waits=[("sIN", xl2)] if j == 0 else ())
            p.E("vector", lambda e, ft=ft, j=j: e.tensor_tensor(
                h[ft][:, j * TC:(j + 1) * TC], h[ft][:, j * TC:(j + 1) * TC], ps[4 + j][:], ALU.mult),
                inc="sVE")
        p.aux[("hh", ft)] = ("sVE", p.n("sVE"))
    ve_h = p.n("sVE")
    p.W("tensor", [("sVE", ve_h)])
    for j in range(NCH):
        p.E("tensor", lambda e, j=j: e.matmul(ps[j][:], onest[0:1, :], r_r01[:, j * TC:(j + 1) * TC],
                                              start=True, stop=True), inc="sPE" if j == NCH - 1 else None)
    pe_bc2 = p.n("sPE")
    for ft in range(NF):
        for j in range(NCH):
            w = [("sPE", pe_bc2)] if (ft == 0 and j == 0) else ()
            p.E("vector", lambda e, ft=ft, j=j: e.tensor_tensor(
                lac[ft][:, j * TC:(j + 1) * TC], h[ft][:, j * TC:(j + 1) * TC], ps[j][:], ALU.mult),
                waits=w)
        p.E("vector", lambda e, ft=ft: e.tensor_scalar(
            out=lac[ft][:], in0=lac[ft][:], scalar1=vps("l1w", ft), scalar2=vps("l1b", ft),
            op0=ALU.mult, op1=ALU.add), inc="sVE")
    ve_ln1 = p.n("sVE")
    for sl in range(8):
        p.slot_rel[sl] = ("sVE", ve_ln1)

    def proj(wsrc, rhs_tiles, kin, nout, evac):
        for o in range(nout):
            bid = o % 2
            w = [p.wbuf_rel[bid]] if bid in p.wbuf_rel else []
            p.E("gpsimd", lambda e, o=o, bid=bid, kin=kin, wsrc=wsrc:
                e.dma_start(out=wbuf[bid][:, 0:kin * 128],
                            in_=wsrc[o * 128:(o + 1) * 128, 0:kin * 128]),
                waits=w, inc="sW", amt=16)
            wrdy = p.n("sW")
            for j in range(NCH):
                slot = (o * NCH + j) % 6
                w2 = [("sW", wrdy)]
                if slot in p.slot_rel:
                    w2.append(p.slot_rel[slot])
                p.W("tensor", w2)
                for kt in range(kin):
                    p.E("tensor", lambda e, kt=kt, j=j, bid=bid, slot=slot, kin=kin:
                        e.matmul(ps[slot][:], wbuf[bid][:, kt * 128:(kt + 1) * 128],
                                 rhs_tiles(kt)[:, j * TC:(j + 1) * TC],
                                 start=(kt == 0), stop=(kt == kin - 1)),
                        inc="sPE" if kt == kin - 1 else None)
                mark = p.n("sPE")
                if j == NCH - 1:
                    p.wbuf_rel[bid] = ("sPE", mark)
                p.slot_rel[slot] = evac(o, j, ps[slot][:], mark)

    def mk_evac_copy(dst):
        def evac(o, j, psap, mark):
            p.E("vector", lambda e, o=o, j=j, psap=psap: e.tensor_copy(
                dst(o)[:, j * TC:(j + 1) * TC], psap), waits=[("sPE", mark)], inc="sVE")
            return ("sVE", p.n("sVE"))
        return evac

    ln1_t = lambda kt: lac[kt][:]
    if DEBUG_L1F:
        for ft in range(NF):
            p.E("gpsimd", lambda e, ft=ft: e.dma_start(out=dbg_l1[ft * 128:(ft + 1) * 128, :], in_=lac[ft][:]),
                waits=[("sVE", ve_ln1)] if ft == 0 else (), inc="sGP", amt=16)
            p.E("gpsimd", lambda e, ft=ft: e.dma_start(out=dbg_h0[ft * 128:(ft + 1) * 128, :], in_=h[ft][:]),
                inc="sGP", amt=16)
        p.W("gpsimd", [("sGP", p.n("sGP"))])
    p.W("tensor", [("sVE", ve_ln1)])
    proj(wq, ln1_t, NF, NF, mk_evac_copy(q_t))
    proj(wkv, ln1_t, NF, 4, mk_evac_copy(lambda o: kcur(o)))
    ve_qkv = p.n("sVE")

    for j in range(4):
        p.E("gpsimd", lambda e, j=j: e.dma_start(out=kvin[j * 128:(j + 1) * 128, :], in_=kcur(j)),
            waits=[("sVE", ve_qkv)] if j == 0 else (), inc="sGP", amt=16)
    gp_kvin = p.n("sGP")
    p.E("gpsimd", lambda e: e.collective_compute(
        "AllGather", ALU.bypass, replica_groups=[list(range(N_CORES))],
        ins=[kvin_t.ap().opt()], outs=[kvout_t.ap().opt()]),
        waits=[("sGP", p.n("sGP"))], inc="sCC")
    p.W("gpsimd", [("sCC", p.n("sCC"))])

    p.W("tensor", [("sVE", ve_qkv)])
    for ft in range(NF):
        par = ft % 2
        w = [("sVE", ve_qkv), ("sGP", gp_kvin)] if ft == 0 else \
            ([("sPE", p.aux[("qn", ft - 2)])] if ft >= 2 else [])
        p.E("scalar", lambda e, ft=ft, par=par: e.square(kcur(2 + par), q_t(ft)), waits=w, inc="sSC")
        sc_s = p.n("sSC")
        p.W("tensor", [("sSC", sc_s)])
        for j in range(NCH):
            p.E("tensor", lambda e, ft=ft, j=j, par=par: e.matmul(
                ps[4 + j][0:16, :], blkq[:, ft * 16:(ft + 1) * 16],
                kcur(2 + par)[:, j * TC:(j + 1) * TC],
                start=(ft == 0), stop=(ft == NF - 1)),
                inc="sPE" if j == NCH - 1 else None)
        p.aux[("qn", ft)] = p.n("sPE")
    pe_qn = p.n("sPE")
    for j in range(NCH):
        p.EV("vector", lambda e, j=j: e.tensor_copy(Mrow()[:, j * TC:(j + 1) * TC], ps[4 + j][0:16, :]),
             waits=[("sPE", pe_qn)] if j == 0 else ())
    p.E("scalar", lambda e: e.activation(out=Mrow(), in_=Mrow(), func=AF.Sqrt),
        waits=[("sVV", p.n("sVV"))], inc="sSC")
    sc_qn = p.n("sSC")

    for s_ in range(8):
        w0 = [("sCC", p.n("sCC"))] if s_ == 0 else [("sVV", p.n("sVV"))]
        for j2 in range(2):
            p.E("sync", lambda e, s_=s_, j2=j2: e.dma_start(
                out=kcur(2 + j2), in_=kvout[s_ * 512 + j2 * 128: s_ * 512 + (j2 + 1) * 128, :]),
                waits=w0 if j2 == 0 else [], inc="sIN", amt=16)
        kl = p.n("sIN")
        for j2 in range(2):
            wsc = [("sIN", kl)]
            if s_ + j2 > 0:
                wsc.append(("sPE", p.aux.get("kn_pe", 0)))
            p.E("scalar", lambda e, j2=j2: e.square(kcur(4 + j2), kcur(2 + j2)), waits=wsc, inc="sSC")
            sc_s = p.n("sSC")
            p.W("tensor", [("sSC", sc_s)])
            for j in range(NCH):
                p.E("tensor", lambda e, j=j, j2=j2: e.matmul(
                    ps[j][0:16, :], blkqk(j2), kcur(4 + j2)[:, j * TC:(j + 1) * TC],
                    start=(j2 == 0), stop=(j2 == 1)),
                    inc="sPE" if (j2 == 1 and j == NCH - 1) else None)
        p.aux["kn_pe"] = p.n("sPE")
        pe_kn = p.n("sPE")
        for j in range(NCH):
            if s_ == 0:
                p.EV("vector", lambda e, j=j, s_=s_: e.tensor_scalar(
                    out=Krow()[:, j * TC:(j + 1) * TC], in0=ps[j][0:16, :],
                    scalar1=maskt[:, s_:s_ + 1], scalar2=None, op0=ALU.mult),
                     waits=[("sPE", pe_kn)] if j == 0 else ())
            else:
                p.EV("vector", lambda e, j=j, s_=s_: e.scalar_tensor_tensor(
                    out=Krow()[:, j * TC:(j + 1) * TC], in0=ps[j][0:16, :],
                    scalar=maskt[:, s_:s_ + 1], in1=Krow()[:, j * TC:(j + 1) * TC],
                    op0=ALU.mult, op1=ALU.max),
                    waits=[("sPE", pe_kn)] if j == 0 else ())
    p.E("scalar", lambda e: e.activation(out=Krow(), in_=Krow(), func=AF.Sqrt),
        waits=[("sVV", p.n("sVV")), ("sSC", sc_qn)], inc="sSC")
    sc_kn = p.n("sSC")
    p.EV("vector", lambda e: e.tensor_mul(Mrow(), Mrow(), Krow()), waits=[("sSC", sc_kn)])
    p.EV("vector", lambda e: e.tensor_scalar_mul(Mrow(), Mrow(), -1.0))
    p.E("vector", lambda e: None, waits=[("sVV", p.n("sVV"))], inc="sVE")
    ve_M = p.n("sVE")

    p.W("tensor", [("sVE", ve_M)])
    for s_ in range(8):
        for j2 in range(2):
            p.E("sync", lambda e, s_=s_, j2=j2: e.dma_start(
                out=kcur(j2), in_=kvout[s_ * 512 + j2 * 128: s_ * 512 + (j2 + 1) * 128, :]),
                waits=[("sPE", p.aux.get("sc_pe", 0))] if j2 == 0 else [], inc="sIN", amt=16)
        kl = p.n("sIN")
        for ft in range(NF):
            par = ft % 2
            wv = [("sIN", kl)] if ft == 0 else []
            if ("pb", par) in p.aux:
                wv.append(p.aux[("pb", par)])
            p.E("vector", lambda e, ft=ft, par=par: e.tensor_tensor(
                kcur(2 + par), q_t(ft), kcur(ft // 4), ALU.mult), waits=wv, inc="sVE")
            ve_p = p.n("sVE")
            p.W("tensor", [("sVE", ve_p)])
            for j in range(NCH):
                p.E("tensor", lambda e, ft=ft, j=j, par=par: e.matmul(
                    ps[j][0:16, :], blkq[:, ft * 16:(ft + 1) * 16],
                    kcur(2 + par)[:, j * TC:(j + 1) * TC],
                    start=(ft == 0), stop=False),
                    inc="sPE" if j == NCH - 1 else None)
            p.aux[("pb", par)] = ("sPE", p.n("sPE"))
        for j in range(NCH):
            p.E("tensor", lambda e, j=j: e.matmul(
                ps[j][0:16, :], identt[0:16, 0:16], Mrow()[:, j * TC:(j + 1) * TC],
                start=False, stop=True),
                inc="sPE" if j == NCH - 1 else None)
        pe_sc = p.n("sPE")
        p.aux["sc_pe"] = pe_sc
        for j in range(NCH):
            p.E("scalar", lambda e, s_=s_, j=j: e.activation(
                out=Erow(s_)[:, j * TC:(j + 1) * TC], in_=ps[j][0:16, :],
                func=AF.Exp, scale=SCLA),
                waits=[("sPE", pe_sc)] if j == 0 else (), inc="sSC" if j == NCH - 1 else None)
    sc_E = p.n("sSC")
    p.W("vector", [("sSC", sc_E)])
    den = Mrow()
    for s_ in range(8):
        p.EV("vector", lambda e, s_=s_: e.tensor_scalar(
            out=Erow(s_), in0=Erow(s_), scalar1=maskt[:, s_:s_ + 1], scalar2=None, op0=ALU.mult))
    for s_ in range(8):
        if s_ == 0:
            p.EV("vector", lambda e, s_=s_: e.tensor_copy(den, Erow(s_)))
        else:
            p.EV("vector", lambda e, s_=s_: e.tensor_tensor(den, den, Erow(s_), ALU.add))
    p.EV("vector", lambda e: e.reciprocal(den, den))
    for s_ in range(8):
        p.EV("vector", lambda e, s_=s_: e.tensor_tensor(Erow(s_), Erow(s_), den, ALU.mult))
    p.E("vector", lambda e: None, waits=[("sVV", p.n("sVV"))], inc="sVE")
    ve_A = p.n("sVE")

    p.W("tensor", [("sVE", ve_A)])
    for s_ in range(8):
        for j2 in range(2):
            p.E("sync", lambda e, s_=s_, j2=j2: e.dma_start(
                out=kcur(j2), in_=kvout[s_ * 512 + 256 + j2 * 128: s_ * 512 + 256 + (j2 + 1) * 128, :]),
                waits=[("sVE", p.aux.get("av_ve", 0) if s_ > 0 else ve_A)] if j2 == 0 else [],
                inc="sIN", amt=16)
        vl = p.n("sIN")
        p.W("vector", [("sIN", vl)])
        for ft in range(NF):
            for j in range(NCH):
                slot = (ft * NCH + j) % 4
                if ("avs", slot) in p.aux:
                    p.W("tensor", [p.aux[("avs", slot)]])
                p.E("tensor", lambda e, s_=s_, ft=ft, j=j, slot=slot: e.matmul(
                    ps[slot][:], blkqT[0:16, ft * 128:(ft + 1) * 128],
                    Erow(s_)[:, j * TC:(j + 1) * TC], start=True, stop=True), inc="sPE")
                pe_b = p.n("sPE")
                par = slot % 2
                p.E("vector", lambda e, ft=ft, j=j, slot=slot, par=par: e.tensor_tensor(
                    xst[par], ps[slot][:], kcur(ft // 4)[:, j * TC:(j + 1) * TC], ALU.mult),
                    waits=[("sPE", pe_b)])
                if s_ == 0:
                    p.E("vector", lambda e, ft=ft, j=j, par=par: e.tensor_copy(
                        lac[ft][:, j * TC:(j + 1) * TC], xst[par]), inc="sVE")
                else:
                    p.E("vector", lambda e, ft=ft, j=j, par=par: e.tensor_tensor(
                        lac[ft][:, j * TC:(j + 1) * TC], lac[ft][:, j * TC:(j + 1) * TC],
                        xst[par], ALU.add), inc="sVE")
                p.aux[("avs", slot)] = ("sVE", p.n("sVE"))
        p.aux["av_ve"] = p.n("sVE")
    ve_att = p.n("sVE")

    if DEBUG_L1F:
        for ft in range(NF):
            p.E("gpsimd", lambda e, ft=ft: e.dma_start(out=dbg_q[ft * 128:(ft + 1) * 128, :], in_=q_t(ft)),
                waits=[("sVE", ve_att)] if ft == 0 else (), inc="sGP", amt=16)
            p.E("gpsimd", lambda e, ft=ft: e.dma_start(out=dbg_acc[ft * 128:(ft + 1) * 128, :], in_=lac[ft][:]),
                inc="sGP", amt=16)
        for j2 in range(4):
            p.E("gpsimd", lambda e, j2=j2: e.dma_start(out=dbg_kv[j2 * 128:(j2 + 1) * 128, :],
                                                       in_=kvout[core * 512 + j2 * 128: core * 512 + (j2 + 1) * 128, :]),
                inc="sGP", amt=16)
        for s_ in range(8):
            p.E("gpsimd", lambda e, s_=s_: e.dma_start(out=dbg_E[:, s_ * T:(s_ + 1) * T], in_=Erow(s_)),
                inc="sGP", amt=16)
        p.W("gpsimd", [("sGP", p.n("sGP"))])

    def evac_x4(o, j, psap, mark):
        p.E("vector", lambda e, o=o, j=j, psap=psap: e.tensor_tensor(
            x4_t(o)[:, j * TC:(j + 1) * TC], psap, h[o][:, j * TC:(j + 1) * TC], ALU.add),
            waits=[("sPE", mark)], inc="sVE")
        return ("sVE", p.n("sVE"))

    p.W("tensor", [("sVE", ve_att)])
    for sl in range(8):
        p.slot_rel[sl] = ("sVE", ve_att)
    proj(wpj, lambda kt: lac[kt][:], NF, NF, evac_x4)
    ve_x4 = p.n("sVE")

    r2_mu = strow2[0:1, 0:T]
    r2_tmp = strow2[0:1, T:2 * T]
    pe_done2 = []
    for ft in range(NF):
        par = ft % 2
        w = [("sVE", ve_x4)] if ft == 0 else []
        if ft >= 2:
            w.append(("sPE", pe_done2[ft - 2]))
        p.E("scalar", lambda e, ft=ft, par=par: e.square(stag[par], x4_t(ft)), waits=w, inc="sSC")
        sc_j = p.n("sSC")
        start, stop = ft == 0, ft == NF - 1
        p.W("tensor", [("sSC", sc_j)] + ([("sVE", ve_x4)] if ft == 0 else []))
        for j in range(NCH):
            p.E("tensor", lambda e, ft=ft, j=j, start=start, stop=stop:
                e.matmul(ps[j][0:1, :], onest[:, 0:1], x4_t(ft)[:, j * TC:(j + 1) * TC],
                         start=start, stop=stop))
            p.E("tensor", lambda e, j=j, par=par, start=start, stop=stop:
                e.matmul(ps[4 + j][0:1, :], onest[:, 0:1], stag[par][:, j * TC:(j + 1) * TC],
                         start=start, stop=stop),
                inc="sPE" if j == NCH - 1 else None)
        pe_done2.append(p.n("sPE"))
    pe_st2 = p.n("sPE")
    for j in range(NCH):
        p.E("vector", lambda e, j=j: e.tensor_scalar_mul(r2_mu[:, j * TC:(j + 1) * TC],
                                                         ps[j][0:1, :], 1.0 / D),
            waits=[("sPE", pe_st2)] if j == 0 else (), inc="sVE" if j == NCH - 1 else None)
    p.E("scalar", lambda e: e.square(r2_tmp, r2_mu), waits=[("sVE", p.n("sVE"))], inc="sSC")
    sc_m22 = p.n("sSC")
    for j in range(NCH):
        p.E("vector", lambda e, j=j: e.scalar_tensor_tensor(
            out=r2_tmp[:, j * TC:(j + 1) * TC], in0=ps[4 + j][0:1, :], scalar=1.0 / D,
            in1=r2_tmp[:, j * TC:(j + 1) * TC], op0=ALU.mult, op1=ALU.subtract),
            waits=[("sSC", sc_m22)] if j == 0 else (), inc="sVE" if j == NCH - 1 else None)
    p.E("scalar", lambda e: e.activation(out=r2_tmp, in_=r2_tmp, func=AF.Ln, bias=epsap),
        waits=[("sVE", p.n("sVE"))], inc="sSC")
    p.E("scalar", lambda e: e.activation(out=r2_tmp, in_=r2_tmp, func=AF.Exp, scale=-0.5), inc="sSC")
    sc_r2 = p.n("sSC")
    p.W("tensor", [("sSC", sc_r2)])
    for j in range(NCH):
        p.E("tensor", lambda e, j=j: e.matmul(ps[j][:], onest[0:1, :], r2_mu[:, j * TC:(j + 1) * TC],
                                              start=True, stop=True))
        p.E("tensor", lambda e, j=j: e.matmul(ps[4 + j][:], onest[0:1, :], r2_tmp[:, j * TC:(j + 1) * TC],
                                              start=True, stop=True), inc="sPE" if j == NCH - 1 else None)
    pe_bc3 = p.n("sPE")
    for ft in range(NF):
        for j in range(NCH):
            w = [("sPE", pe_bc3)] if (ft == 0 and j == 0) else ()
            p.E("vector", lambda e, ft=ft, j=j: e.tensor_tensor(
                h[ft][:, j * TC:(j + 1) * TC], x4_t(ft)[:, j * TC:(j + 1) * TC], ps[j][:], ALU.subtract),
                waits=w)
            p.E("vector", lambda e, ft=ft, j=j: e.tensor_tensor(
                h[ft][:, j * TC:(j + 1) * TC], h[ft][:, j * TC:(j + 1) * TC], ps[4 + j][:], ALU.mult))
        p.E("vector", lambda e, ft=ft: e.tensor_scalar(
            out=h[ft][:], in0=h[ft][:], scalar1=vps("l2w", ft), scalar2=vps("l2b", ft),
            op0=ALU.mult, op1=ALU.add), inc="sVE")
    ve_ln2 = p.n("sVE")
    for sl in range(8):
        p.slot_rel[sl] = ("sVE", ve_ln2)

    if DEBUG_L1F:
        for ft in range(NF):
            p.E("gpsimd", lambda e, ft=ft: e.dma_start(out=dbg_x4[ft * 128:(ft + 1) * 128, :], in_=x4_t(ft)),
                waits=[("sVE", ve_ln2)] if ft == 0 else (), inc="sGP", amt=16)
            p.E("gpsimd", lambda e, ft=ft: e.dma_start(out=dbg_ln2[ft * 128:(ft + 1) * 128, :], in_=h[ft][:]),
                inc="sGP", amt=16)
        p.W("gpsimd", [("sGP", p.n("sGP"))])

    p.E("sync", lambda e: e.dma_start(out=rw[:], in_=rwp), inc="sIN", amt=16)
    w_rw = p.n("sIN")
    p.W("tensor", [("sVE", ve_ln2)])
    for kt in range(NF):
        p.EV("vector", lambda e, kt=kt: e.tensor_scalar(
            out=ohw[:], in0=rw[:, kt * 16:(kt + 1) * 16], scalar1=vps("l2b", kt),
            scalar2=None, op0=ALU.mult), waits=[("sIN", w_rw)] if kt == 0 else ())
        vv_b = p.n("sVV")
        p.E("tensor", lambda e, kt=kt: e.matmul(ps[6][0:1, 0:16], onest[:, 0:1], ohw[:],
                                                start=(kt == 0), stop=(kt == NF - 1)),
            waits=[("sVV", vv_b)], inc="sPE" if kt == NF - 1 else None)
    pe_s2 = p.n("sPE")
    for kt in range(NF):
        p.EV("vector", lambda e, kt=kt: e.tensor_scalar(
            out=rw[:, kt * 16:(kt + 1) * 16], in0=rw[:, kt * 16:(kt + 1) * 16],
            scalar1=vps("l2w", kt), scalar2=None, op0=ALU.mult),
            waits=[("sPE", pe_s2)] if kt == 0 else ())
    vv_w = p.n("sVV")
    p.W("tensor", [("sVV", vv_w)])
    for kt in range(NF):
        p.E("tensor", lambda e, kt=kt: e.matmul(ps[7][0:1, 0:16], onest[:, 0:1],
                                                rw[:, kt * 16:(kt + 1) * 16],
                                                start=(kt == 0), stop=(kt == NF - 1)),
            inc="sPE" if kt == NF - 1 else None)
    pe_s1 = p.n("sPE")
    p.EV("vector", lambda e: e.tensor_copy(ctrow3[0:1, 0:16], ps[6][0:1, 0:16]),
         waits=[("sPE", pe_s1)])
    vv_s2r = p.n("sVV")
    p.W("tensor", [("sVV", vv_s2r)])
    p.E("tensor", lambda e: e.matmul(ps[6][:, 0:16], onest[0:1, :], ctrow3[0:1, 0:16],
                                     start=True, stop=True), inc="sPE")
    pe_b2 = p.n("sPE")
    p.EV("vector", lambda e: e.tensor_copy(s2b[:], ps[6][:, 0:16]), waits=[("sPE", pe_b2)])
    p.EV("vector", lambda e: e.tensor_copy(ctrow3[0:1, 0:16], ps[7][0:1, 0:16]))
    vv_s1r = p.n("sVV")
    p.W("tensor", [("sVV", vv_s1r)])
    p.E("tensor", lambda e: e.matmul(ps[7][:, 0:16], onest[0:1, :], ctrow3[0:1, 0:16],
                                     start=True, stop=True), inc="sPE")
    pe_b1 = p.n("sPE")
    p.EV("vector", lambda e: e.tensor_copy(s1b[:], ps[7][:, 0:16]), waits=[("sPE", pe_b1)])
    vv_s1b = p.n("sVV")

    for tb in range(16):
        slot = tb % 2
        sl_t = 2 + tb % 2
        wt = [p.aux[("mr", sl_t)]] if ("mr", sl_t) in p.aux else [("sVV", vv_s1b)]
        p.W("tensor", wt)
        p.E("tensor", lambda e, tb=tb, sl_t=sl_t: e.transpose(
            ps[sl_t][0:128, 0:1], r2_mu[0:1, tb * 128:(tb + 1) * 128], identt[0:1, 0:1]))
        p.E("tensor", lambda e, tb=tb, sl_t=sl_t: e.transpose(
            ps[sl_t][0:128, 1:2], r2_tmp[0:1, tb * 128:(tb + 1) * 128], identt[0:1, 0:1]),
            inc="sPE")
        pe_t = p.n("sPE")
        p.EV("vector", lambda e, sl_t=sl_t: e.tensor_copy(murc[:, 0:2], ps[sl_t][0:128, 0:2]),
             waits=[("sPE", pe_t)])
        p.aux[("mr", sl_t)] = ("sVV", p.n("sVV"))
        if slot in p.slot_rel:
            p.W("tensor", [p.slot_rel[slot]])
        p.W("tensor", [p.aux[("mr", sl_t)]])
        for kt in range(NF):
            p.E("tensor", lambda e, tb=tb, kt=kt, slot=slot:
                e.matmul(ps[slot][:, 0:16], x4_t(kt)[:, tb * 128:(tb + 1) * 128],
                         rw[:, kt * 16:(kt + 1) * 16], start=(kt == 0), stop=(kt == NF - 1)),
                inc="sPE" if kt == NF - 1 else None)
        pe_l = p.n("sPE")
        p.EV("vector", lambda e: e.tensor_scalar(
            out=ohw[:], in0=s1b[:], scalar1=murc[:, 0:1], scalar2=None, op0=ALU.mult),
            waits=[("sPE", pe_l)])
        p.EV("vector", lambda e, slot=slot: e.tensor_tensor(lgt[:], ps[slot][:, 0:16], ohw[:], ALU.subtract))
        p.EV("vector", lambda e: e.tensor_scalar(
            out=lgt[:], in0=lgt[:], scalar1=murc[:, 1:2], scalar2=None, op0=ALU.mult))
        p.EV("vector", lambda e: e.tensor_tensor(lgt[:], lgt[:], s2b[:], ALU.add))
        vv_lg = p.n("sVV")
        p.slot_rel[slot] = ("sVV", vv_lg)
        p.EV("vector", lambda e: e.reduce_max(mrow[:, 0:1], lgt[:], axis=AX.X))
        p.EV("vector", lambda e: e.tensor_scalar_mul(mrow[:, 1:2], mrow[:, 0:1], -1.0))
        vv_neg = p.n("sVV")
        p.E("scalar", lambda e, tb=tb: e.activation(
            out=prb[:, tb * 16:(tb + 1) * 16], in_=lgt[:], func=AF.Exp,
            bias=mrow[:, 1:2]), waits=[("sVV", vv_neg)], inc="sSC")
        sc_e = p.n("sSC")
        p.EV("vector", lambda e, tb=tb: e.reduce_sum(mrow[:, 2:3], prb[:, tb * 16:(tb + 1) * 16], axis=AX.X),
             waits=[("sSC", sc_e)])
        p.EV("vector", lambda e: e.reciprocal(mrow[:, 3:4], mrow[:, 2:3]))
        p.EV("vector", lambda e, tb=tb: e.tensor_scalar(
            out=prb[:, tb * 16:(tb + 1) * 16], in0=prb[:, tb * 16:(tb + 1) * 16],
            scalar1=mrow[:, 3:4], scalar2=None, op0=ALU.mult))
        tmp_tb = cwall[:, tb * 16:(tb + 1) * 16]
        p.EV("vector", lambda e, tb=tb, tmp_tb=tmp_tb: e.tensor_copy(tmp_tb, prb[:, tb * 16:(tb + 1) * 16]))
        tmp_tb = cwall[:, tb * 16:(tb + 1) * 16]
        p.EV("vector", lambda e, tb=tb, tmp_tb=tmp_tb: e.tensor_copy(tmp_tb, prb[:, tb * 16:(tb + 1) * 16]))
        for k in range(2):
            ohk_dst = oh0 if k == 0 else oh1
            p.EV("vector", lambda e, tmp_tb=tmp_tb: e.reduce_max(mrow[:, 0:1], tmp_tb, axis=AX.X))
            p.EV("vector", lambda e, tmp_tb=tmp_tb: e.tensor_scalar(
                out=msk[:], in0=tmp_tb, scalar1=mrow[:, 0:1], scalar2=None, op0=ALU.is_ge))
            p.EV("vector", lambda e: e.select(out=ohk[:], mask=msk[:], on_true=iot[:], on_false=big[:]))
            p.EV("vector", lambda e: e.tensor_reduce(mrow[:, 1:2], ohk[:], axis=AX.X, op=ALU.min))
            p.EV("vector", lambda e, tb=tb, ohk_dst=ohk_dst: e.tensor_scalar(
                out=ohk_dst[:, tb * 16:(tb + 1) * 16], in0=iot[:], scalar1=mrow[:, 1:2],
                scalar2=None, op0=ALU.is_equal))
            p.EV("vector", lambda e, tb=tb, ohk_dst=ohk_dst: e.tensor_tensor(
                ohw[:], ohk_dst[:, tb * 16:(tb + 1) * 16], prb[:, tb * 16:(tb + 1) * 16], ALU.mult))
            p.EV("vector", lambda e, tb=tb, k=k: e.reduce_sum(gat[:, k * 16 + tb: k * 16 + tb + 1],
                                                              ohw[:], axis=AX.X))
            p.EV("vector", lambda e, tmp_tb=tmp_tb, tb=tb, ohk_dst=ohk_dst: e.scalar_tensor_tensor(
                out=tmp_tb, in0=ohk_dst[:, tb * 16:(tb + 1) * 16], scalar=-1e30, in1=tmp_tb,
                op0=ALU.mult, op1=ALU.add))
    p.E("vector", lambda e: None, waits=[("sVV", p.n("sVV"))], inc="sVE")
    ve_top2 = p.n("sVE")

    p.W("tensor", [("sVE", ve_top2)])
    for tb in range(16):
        sl = 6 + tb % 2
        p.EV("vector", lambda e, tb=tb: e.tensor_tensor(
            ohw[:], oh0[:, tb * 16:(tb + 1) * 16], oh1[:, tb * 16:(tb + 1) * 16], ALU.add))
        vv_i = p.n("sVV")
        if ("zt", sl) in p.aux:
            p.W("tensor", [p.aux[("zt", sl)]])
        p.E("tensor", lambda e, sl=sl: e.transpose(ps[sl][0:16, 0:128], ohw[:], identt[:]),
            waits=[("sVV", vv_i)], inc="sPE")
        pe_t = p.n("sPE")
        p.EV("vector", lambda e, tb=tb, sl=sl: e.tensor_copy(Zrow[:, tb * 128:(tb + 1) * 128],
                                                             ps[sl][0:16, 0:128]),
             waits=[("sPE", pe_t)])
        p.aux[("zt", sl)] = ("sVV", p.n("sVV"))
    p.EV("vector", lambda e: e.memset(Zzero, 0.0))
    p.EV("vector", lambda e: e.tensor_tensor_scan(ZInc, Zzero, Zrow, 0.0, ALU.max, ALU.add))
    p.EV("vector", lambda e: e.tensor_tensor(Zrow, ZInc, Zrow, ALU.subtract))
    p.EV("vector", lambda e: e.tensor_copy(ctrow[:], ZInc[:, T - 1:T]))
    p.E("vector", lambda e: None, waits=[("sVV", p.n("sVV"))], inc="sVE")
    ve_Z = p.n("sVE")
    p.E("gpsimd", lambda e: e.dma_start(out=ctin, in_=ctrow[:]), waits=[("sVE", ve_Z)], inc="sGP", amt=16)
    p.E("gpsimd", lambda e: e.collective_compute(
        "AllGather", ALU.bypass, replica_groups=[list(range(N_CORES))],
        ins=[ctin_t.ap().opt()], outs=[ctout_t.ap().opt()]), waits=[("sGP", p.n("sGP"))], inc="sCC")
    cc2 = p.n("sCC")
    p.E("gpsimd", lambda e: e.dma_start(out=ctsb[:], in_=ctout), waits=[("sCC", cc2)], inc="sGP", amt=16)
    gp_ct = p.n("sGP")
    p.W("tensor", [("sGP", gp_ct)])
    p.E("tensor", lambda e: e.matmul(ps[6][0:16, 0:1], pfx[:], ctsb[:], start=True, stop=True), inc="sPE")
    pe_of = p.n("sPE")
    p.EV("vector", lambda e: e.tensor_copy(ctrow2[0:16, 0:1], ps[6][0:16, 0:1]), waits=[("sPE", pe_of)])
    vv_of = p.n("sVV")
    p.E("tensor", lambda e: e.transpose(ps[7][0:1, 0:16], ctrow2[0:16, 0:1], identt[0:16, 0:16]),
        waits=[("sVV", vv_of)], inc="sPE")
    pe_of2 = p.n("sPE")
    p.EV("vector", lambda e: e.tensor_copy(ctrow3[0:1, 0:16], ps[7][0:1, 0:16]), waits=[("sPE", pe_of2)])
    vv_of3 = p.n("sVV")
    p.W("tensor", [("sVV", vv_of3)])
    p.E("tensor", lambda e: e.matmul(ps[6][:, 0:16], onest[0:1, :], ctrow3[0:1, 0:16],
                                     start=True, stop=True), inc="sPE")
    pe_ofb = p.n("sPE")
    p.EV("vector", lambda e: e.tensor_copy(offb[:], ps[6][:, 0:16]), waits=[("sPE", pe_ofb)])
    vv_offb = p.n("sVV")

    for tb in range(16):
        sl = 6 + tb % 2
        if ("pz", sl) in p.aux:
            p.W("tensor", [p.aux[("pz", sl)]])
        else:
            p.W("tensor", [("sVV", vv_offb)])
        p.E("tensor", lambda e, tb=tb, sl=sl: e.transpose(
            ps[sl][0:128, 0:16], Zrow[:, tb * 128:(tb + 1) * 128], identt[0:16, 0:16]),
            inc="sPE")
        pe_z = p.n("sPE")
        p.EV("vector", lambda e, tb=tb, sl=sl: e.tensor_copy(exT[:], ps[sl][0:128, 0:16]),
             waits=[("sPE", pe_z)])
        p.aux[("pz", sl)] = ("sVV", p.n("sVV"))
        p.EV("vector", lambda e, tb=tb: e.memset(cwall[:, tb * 16:(tb + 1) * 16], 0.0))
        for k in range(2):
            ohk_src = oh0 if k == 0 else oh1
            p.EV("vector", lambda e, tb=tb, ohk_src=ohk_src: e.tensor_tensor(
                ohw[:], ohk_src[:, tb * 16:(tb + 1) * 16], exT[:], ALU.mult))
            p.EV("vector", lambda e: e.reduce_sum(mrow[:, 0:1], ohw[:], axis=AX.X))
            p.EV("vector", lambda e, tb=tb, ohk_src=ohk_src: e.tensor_tensor(
                ohw[:], ohk_src[:, tb * 16:(tb + 1) * 16], offb[:], ALU.mult))
            p.EV("vector", lambda e: e.reduce_sum(mrow[:, 1:2], ohw[:], axis=AX.X))
            p.EV("vector", lambda e: e.tensor_tensor(mrow[:, 0:1], mrow[:, 0:1], mrow[:, 1:2], ALU.add))
            p.EV("vector", lambda e: e.tensor_scalar(
                out=mrow[:, 2:3], in0=mrow[:, 0:1], scalar1=CAP, scalar2=None, op0=ALU.is_lt))
            p.EV("vector", lambda e, tb=tb, k=k: e.tensor_tensor(
                mrow[:, 3:4], gat[:, k * 16 + tb: k * 16 + tb + 1], mrow[:, 2:3], ALU.mult))
            p.EV("vector", lambda e, tb=tb, ohk_src=ohk_src: e.scalar_tensor_tensor(
                out=cwall[:, tb * 16:(tb + 1) * 16], in0=ohk_src[:, tb * 16:(tb + 1) * 16],
                scalar=mrow[:, 3:4], in1=cwall[:, tb * 16:(tb + 1) * 16],
                op0=ALU.mult, op1=ALU.add))
    p.E("vector", lambda e: None, waits=[("sVV", p.n("sVV"))], inc="sVE")
    ve_cw = p.n("sVE")

    if DEBUG_L1F:
        p.E("gpsimd", lambda e: e.dma_start(out=dbg_cw, in_=cwall[:]),
            waits=[("sVE", ve_cw)], inc="sGP", amt=16)
        p.E("gpsimd", lambda e: e.dma_start(out=dbg_off, in_=offb[:]), inc="sGP", amt=16)
        p.W("gpsimd", [("sGP", p.n("sGP"))])

    for tb in range(16):
        sl = 6 + tb % 2
        p.W("tensor", [p.aux[("pz", sl)]] if tb < 2 else [p.aux[("ct", sl)]])
        p.E("tensor", lambda e, tb=tb, sl=sl: e.transpose(
            ps[sl][0:16, 0:128], cwall[:, tb * 16:(tb + 1) * 16], identt[:]),
            waits=[("sVE", ve_cw)] if tb == 0 else [], inc="sPE")
        pe_tr = p.n("sPE")
        p.EV("vector", lambda e, tb=tb, sl=sl: e.tensor_copy(
            cwT[:, tb * 128:(tb + 1) * 128], ps[sl][0:16, 0:128]), waits=[("sPE", pe_tr)])
        p.aux[("ct", sl)] = ("sVV", p.n("sVV"))
    p.E("vector", lambda e: None, waits=[("sVV", p.n("sVV"))], inc="sVE")
    ve_cwT = p.n("sVE")
    p.slot_rel[6] = ("sVE", ve_cwT)
    p.slot_rel[7] = ("sVE", ve_cwT)

    def load_w1(e_, wait=None):
        for it in range(NI1):
            p.E("gpsimd", lambda e, e_=e_, it=it: e.dma_start(
                out=bg1s(it), in_=w1[e_, it * 128:(it + 1) * 128, :]),
                waits=[wait] if (wait and it == 0) else (), inc="sW", amt=16)
        return p.n("sW")

    def load_w2(e_, wait=None):
        for o in range(NF):
            p.E("gpsimd", lambda e, e_=e_, o=o: e.dma_start(
                out=bd2s(o), in_=w2[e_, o * 128:(o + 1) * 128, :]),
                waits=[wait] if (wait and o == 0) else (), inc="sW", amt=16)
        return p.n("sW")

    wrdy_g = load_w1(0, wait=("sVE", ve_x4))
    wrdy_d = load_w2(0, wait=("sCC", cc2))
    for e_ in range(E):
        p.W("tensor", [("sVE", ve_cwT)] if e_ == 0 else [])
        for j in range(NCH):
            sl = 6 + j % 2
            if ("cwb", sl) in p.aux:
                p.W("tensor", [p.aux[("cwb", sl)]])
            p.E("tensor", lambda e, j=j, e_=e_, sl=sl: e.matmul(
                ps[sl][:], oh16[:, e_ * 128:(e_ + 1) * 128],
                cwT[0:16, j * TC:(j + 1) * TC], start=True, stop=True), inc="sPE")
            pe_b = p.n("sPE")
            p.E("vector", lambda e, j=j, sl=sl: e.tensor_copy(cwb[:, j * TC:(j + 1) * TC],
                                                              ps[sl][:]),
                waits=[("sPE", pe_b)], inc="sVE")
            p.aux[("cwb", sl)] = ("sVE", p.n("sVE"))
        ve_cwb = p.n("sVE")
        p.W("tensor", [("sW", wrdy_g)])
        for n in range(NCH):
            for it in range(NI1):
                sg_ = (n * NI1 + it) % 2
                slot_g = sg_
                wmm = []
                if slot_g in p.slot_rel:
                    wmm.append(p.slot_rel[slot_g])
                if wmm:
                    p.W("tensor", wmm)
                for kt in range(NF):
                    p.E("tensor", lambda e, it=it, kt=kt, n=n, slot_g=slot_g:
                        e.matmul(ps[slot_g][:], bg1k(it, kt),
                                 h[kt][:, n * TC:(n + 1) * TC], start=(kt == 0), stop=(kt == NF - 1)),
                        inc="sPE" if kt == NF - 1 else None)
                pe_g = p.n("sPE")
                if n == NCH - 1 and it == NI1 - 1:
                    p.aux["g_done"] = pe_g
                wact = [("sPE", pe_g), ("sVE", p.n("sVE"))]
                if "dn2" in p.aux:
                    wact.append(("sPE", p.aux["dn2"]))
                p.E("scalar", lambda e, it=it, slot_g=slot_g: e.activation(
                    out=act(it), in_=ps[slot_g][:], func=AF.Gelu_apprx_tanh),
                    waits=wact, inc="sSC")
                sc_a = p.n("sSC")
                p.slot_rel[slot_g] = ("sSC", sc_a)
                p.E("vector", lambda e, it=it, n=n: e.tensor_tensor(
                    act(it), act(it), cwb[:, n * TC:(n + 1) * TC], ALU.mult),
                    waits=[("sSC", sc_a)], inc="sVE")
            ve_act = p.n("sVE")
            p.W("tensor", [("sW", wrdy_d), ("sVE", ve_act)])
            for o in range(NF):
                slot = 4 + o % 2
                if slot in p.slot_rel:
                    p.W("tensor", [p.slot_rel[slot]])
                for kt in range(NI1):
                    p.E("tensor", lambda e, o=o, kt=kt, slot=slot:
                        e.matmul(ps[slot][:], bd2k(o, kt), act(kt),
                                 start=(kt == 0), stop=(kt == NI1 - 1)),
                        inc="sPE" if kt == NI1 - 1 else None)
                pe_d = p.n("sPE")
                p.E("vector", lambda e, o=o, n=n, slot=slot: e.tensor_tensor(
                    x4_t(o)[:, n * TC:(n + 1) * TC], x4_t(o)[:, n * TC:(n + 1) * TC],
                    ps[slot][:], ALU.add),
                    waits=[("sPE", pe_d)], inc="sVE")
                p.slot_rel[slot] = ("sVE", p.n("sVE"))
            p.aux["dn2"] = p.n("sPE")
        if e_ < E - 1:
            wrdy_g = load_w1(e_ + 1, wait=("sPE", p.aux["g_done"]))
            wrdy_d = load_w2(e_ + 1, wait=("sPE", p.n("sPE")))
    ve_moe = p.n("sVE")

    for o in range(NF):
        for j in range(NCH):
            par = (o * NCH + j) % 2
            wx = [p.aux[("xi", par)]] if ("xi", par) in p.aux else [("sVE", ve_moe)]
            p.E("sync", lambda e, o=o, j=j, par=par: e.dma_start(
                out=xio[par], in_=x3[o * 128:(o + 1) * 128, j * TC:(j + 1) * TC]),
                waits=wx, inc="sIN", amt=16)
            xrdy = p.n("sIN")
            wv = [("sIN", xrdy), ("sVE", ve_moe)]
            if ("xo", par) in p.aux:
                wv.append(p.aux[("xo", par)])
            p.E("vector", lambda e, o=o, j=j, par=par: e.tensor_tensor(
                xst[par], x4_t(o)[:, j * TC:(j + 1) * TC], xio[par], ALU.add),
                waits=wv, inc="sVE")
            ve_o = p.n("sVE")
            p.aux[("xi", par)] = ("sVE", ve_o)
            p.E("gpsimd", lambda e, o=o, j=j, par=par: e.dma_start(
                out=out_d[o * 128:(o + 1) * 128, j * TC:(j + 1) * TC], in_=xst[par]),
                waits=[("sVE", ve_o)], inc="sOUT", amt=16)
            p.aux[("xo", par)] = ("sOUT", p.n("sOUT"))
    p.W("gpsimd", [("sOUT", p.n("sOUT"))])
    p.W("sync", [("sOUT", p.n("sOUT"))])

    p.finalize()
    return nc, es





def l1_ffn_inputs(core, inp, x3_fm):
    E, NI1 = 16, 16
    PQ = np.zeros(1024, np.int64)
    for ft in range(8):
        PQ[ft * 128: ft * 128 + 64] = 64 * QHEAD0[ft] + np.arange(64)
        PQ[ft * 128 + 64: ft * 128 + 128] = 64 * QHEAD1[ft] + np.arange(64)
    qkvw = f32(inp["L1_ffn_qkv_weight"])
    Wq = qkvw[0:1024].T[:, PQ]
    Wkv = qkvw[1024:1536].T
    Wp = f32(inp["L1_ffn_proj_weight"]).T[PQ, :]
    w1 = np.stack([wprep(f32(inp["L1_ffn_moe_w1"][e]).T, NF, NI1) for e in range(E)])
    w2 = np.stack([wprep(f32(inp["L1_ffn_moe_w2"][e]).T, NI1, NF) for e in range(E)])
    rwp = np.ascontiguousarray(
        f32(inp["L1_ffn_router_weight"]).T.reshape(NF, 128, 16).transpose(1, 0, 2).reshape(128, NF * 16))
    blkq = np.zeros((128, NF * 16), np.float32)
    blkqT = np.zeros((16, NF * 128), np.float32)
    for ft in range(8):
        blkq[0:64, ft * 16 + QHEAD0[ft]] = 1.0
        blkq[64:128, ft * 16 + QHEAD1[ft]] = 1.0
        blkqT[QHEAD0[ft], ft * 128: ft * 128 + 64] = 1.0
        blkqT[QHEAD1[ft], ft * 128 + 64: ft * 128 + 128] = 1.0
    blkk = np.zeros((128, 2 * 16), np.float32)
    for j2 in range(2):
        for i in range(4):
            blkk[0:64, j2 * 16 + 4 * (2 * j2) + i] = 1.0
            blkk[64:128, j2 * 16 + 4 * (2 * j2 + 1) + i] = 1.0
    cmask = np.zeros((16, 8), np.float32)
    cmask[:, :core + 1] = 1.0
    cpfx = np.zeros((128, 16), np.float32)
    for cc in range(core):
        for ee in range(16):
            cpfx[cc * 16 + ee, ee] = 1.0
    vpk = np.zeros((128, 40), np.float32)
    for i, nm in enumerate(["L1_ffn_ln1_weight", "L1_ffn_ln1_bias", "L1_ffn_ln2_weight", "L1_ffn_ln2_bias"]):
        vpk[:, i * 8:(i + 1) * 8] = f32(inp[nm]).reshape(NF, 128).T
    vpk[:, 32] = 1e-5
    return {
        "x3": x3_fm[core],
        "wq": wprep(Wq, NF, NF),
        "wkv": wprep(Wkv, NF, 4),
        "wpj": wprep(Wp, NF, NF),
        "w1": w1, "w2": w2, "rwp": rwp,
        "vpack": vpk,
        "cones": np.ones((128, 128), np.float32),
        "cident": np.eye(128, dtype=np.float32),
        "ciota": np.tile(np.arange(16, dtype=np.float32), (128, 1)),
        "cbig": np.full((128, 16), 1e30, np.float32),
        "coh16": np.concatenate([np.tile((np.arange(16) == e).astype(np.float32)[:, None], (1, 128))
                                 for e in range(16)], axis=1),
        "cblkq": blkq, "cblkqT": blkqT, "cblkk": blkk,
        "cmask": cmask, "cpfx": cpfx,
    }


def run_l1_ffn(inp, x3_fm):
    nc, es = build_l1_ffn(0)
    in_maps = [l1_ffn_inputs(c, inp, x3_fm) for c in range(N_CORES)]
    res = run_bass_kernel_spmd(nc, in_maps, core_ids=list(range(N_CORES)))
    es.close()
    if DEBUG_L1F:
        return res
    return [res.results[c]["out"] for c in range(N_CORES)]


def kernel(**inputs):
    x1_fm = run_l0_attn(inputs)
    x2_fm = run_l0_ffn(inputs, x1_fm)
    x3_fm = run_l1_attn(inputs, x2_fm)
    out_fm = run_l1_ffn(inputs, x3_fm)
    out = np.stack([out_fm[c].T for c in range(N_CORES)], axis=0)
    return np.ascontiguousarray(out, np.float32)
